# version 05 (final)
# baseline (speedup 1.0000x reference)
import numpy as np
import ml_dtypes
import concourse.bass as bass
import concourse.mybir as mybir
from concourse.bass_utils import run_bass_kernel_spmd
from contextlib import ExitStack

F32 = mybir.dt.float32
BF16 = mybir.dt.bfloat16
AF = mybir.ActivationFunctionType
ALU = mybir.AluOpType
ENGINES = ("sync", "vector", "scalar", "tensor", "gpsimd")


class Sem:
    def __init__(self, handle, name):
        self.h = handle
        self.name = name
        self.n = 0


class Emit:
    def __init__(self, nc):
        self.nc = nc
        self.ops = {e: [] for e in ENGINES}
        self._vchain = None
        self._last_v = None

    def enable_vchain(self, sem):
        self._vchain = sem

    def op(self, engine, fn, waits=(), inc=None):
        if engine == "vector" and self._vchain is not None:
            waits = list(waits)
            if self._last_v is not None:
                waits.append(self._last_v)
            if inc is None:
                inc = (self._vchain, 1)
            self._last_v = (inc[0], inc[0].n + inc[1])
        return self._op(engine, fn, waits, inc)

    def _op(self, engine, fn, waits=(), inc=None):
        for (sem, val) in waits:
            if val > 0:
                self.ops[engine].append(
                    lambda eng, s=sem, v=val: eng.wait_ge(s.h, v))
        if inc is not None:
            sem, k = inc
            sem.n += k

            def f(eng, fn=fn, s=sem, k=k):
                fn(eng).then_inc(s.h, k)
            self.ops[engine].append(f)
            return sem.n
        self.ops[engine].append(fn)
        return None

    def wait(self, engine, sem, val):
        if val > 0:
            self.ops[engine].append(lambda eng, s=sem, v=val: eng.wait_ge(s.h, v))

    def barrier(self, bar):
        for e in ENGINES:
            self.op(e, lambda eng: eng.nop(), inc=(bar, 1))
        tgt = bar.n
        for e in ENGINES:
            self.wait(e, bar, tgt)

    def replay(self, block):
        for e in ENGINES:
            fns = self.ops[e]
            if not fns:
                continue

            def body(eng, fns=fns):
                for f in fns:
                    f(eng)
            getattr(block, e)(body)


def col_layout(W, tile_m=128):
    K, M = W.shape
    kc, mc = K // 128, M // tile_m
    A = W.reshape(kc, 128, mc, tile_m).transpose(2, 1, 0, 3)
    return np.ascontiguousarray(A.reshape(mc, 128, kc * tile_m).astype(np.float32))


def fm_layout(X):
    D, T = X.shape
    kc = D // 128
    return np.ascontiguousarray(
        X.reshape(kc, 128, T).transpose(1, 0, 2).reshape(128, kc * T).astype(np.float32))


def fm_unlayout(A, D, T):
    kc = D // 128
    return A.reshape(128, kc, T).transpose(1, 0, 2).reshape(D, T)


def vec_cols(v):
    F = v.shape[0]
    return np.ascontiguousarray(v.reshape(F // 128, 128).T.astype(np.float32))


def diag_stage(w):
    C, S = w.shape
    cc = C // 128
    A = np.zeros((128, S, cc, 128), np.float32)
    p = np.arange(128)
    for s in range(S):
        for c in range(cc):
            A[p, s, c, p] = w[c * 128 + p, s]
    return np.ascontiguousarray(A.reshape(128, S * cc * 128))


MASK_TRILNEG = -np.tril(np.ones((128, 128), np.float32), -1)
MASK_TRIUNEG = -np.triu(np.ones((128, 128), np.float32), 1)
MASK_CAUS_T = np.triu(np.ones((128, 128), np.float32), 0)
MASK_LOWBIG = np.where(np.tril(np.ones((128, 128), np.float32), -1) > 0,
                       np.float32(-3e38) / 4, np.float32(0.0))
EYE128 = np.eye(128, dtype=np.float32)


def build_L1(ngroups=16, debug=False):
    T = ngroups * 512
    nc = bass.Bass("TRN2", target_bir_lowering=False, detect_race_conditions=False)

    def dram(name, shape, dt=F32, kind="ExternalInput"):
        return nc.dram_tensor(name, shape, dt, kind=kind).ap()

    x_d = dram("x", [128, 8 * 8192])
    wq_d = dram("wq", [2, 128, 1024])
    wk_d = dram("wk", [2, 128, 1024])
    wv_d = dram("wv", [2, 128, 1024])
    wo_d = dram("wo", [8, 128, 256])
    wb_d = dram("wb", [128, 8])
    wmix_d = dram("wmix", [128, 8])
    cq_d = dram("cq", [128, 1024])
    ck_d = dram("ck", [128, 1024])
    cv_d = dram("cv", [128, 1024])
    nw_d = dram("nw", [128, 2])
    mtril_d = dram("mtril", [128, 128])
    mtriu_d = dram("mtriu", [128, 128])
    mcaus_d = dram("mcaus", [128, 128])
    eye_d = dram("eye", [128, 128])
    y_d = dram("ypart", [128, 8 * T], F32, kind="ExternalOutput")
    dbg = {}
    if debug:
        for nm, w in [("d_xb", 8*512), ("d_bA", 512), ("d_bC", 512),
                      ("d_nx", 8*512), ("d_qs", 2*512), ("d_ks", 2*512), ("d_vs", 2*512),
                      ("d_qn", 2*512), ("d_kn", 2*512), ("d_kbt", 2*512), ("d_vbt", 2*512),
                      ("d_Y0", 512), ("d_atnT", 128), ("d_ktok", 256), ("d_wfm", 256),
                      ("d_ogrp", 2*512), ("d_o3g", 2*512)]:
            dbg[nm] = nc.dram_tensor(nm, [128, w], BF16, kind="ExternalOutput").ap()
        dbg["d_rows"] = nc.dram_tensor("d_rows", [1, 10*512], F32, kind="ExternalOutput").ap()

    es = ExitStack()
    sbt = lambda n_, s_, dt=F32: es.enter_context(nc.sbuf_tensor(n_, s_, dt))
    pst = lambda n_, s_, dt=F32: es.enter_context(nc.psum_tensor(n_, s_, dt))

    wq = sbt("wq_s", [128, 2048], BF16)
    wk = sbt("wk_s", [128, 2048], BF16)
    wv = sbt("wv_s", [128, 2048], BF16)
    wo = sbt("wo_s", [128, 2048], BF16)
    wb = sbt("wb_s", [128, 8], BF16)
    wmix = sbt("wmix_s", [128, 8], BF16)
    cdq = sbt("cdq_s", [128, 1024], BF16)
    cdk = sbt("cdk_s", [128, 1024], BF16)
    cdv = sbt("cdv_s", [128, 1024], BF16)
    nwc = sbt("nwc_s", [128, 2])
    mtril = sbt("mtril_s", [128, 128])
    mtriu = sbt("mtriu_s", [128, 128])
    mcaus = sbt("mcaus_s", [128, 128])
    eyeb = sbt("eyeb_s", [128, 128], BF16)
    ones_c = sbt("ones_c", [128, 1], BF16)
    ones_r = sbt("ones_r", [1, 128], BF16)

    xg = sbt("xg", [128, 8 * 512])
    xb = sbt("xb", [128, 8 * 512], BF16)
    nx = sbt("nx", [128, 8 * 512], BF16)
    qpre = sbt("qpre", [128, 2 * 515], BF16)
    kpre = sbt("kpre", [128, 2 * 515], BF16)
    vpre = sbt("vpre", [128, 2 * 515], BF16)
    qs = sbt("qs", [128, 2 * 512], BF16)
    ks = sbt("ks", [128, 2 * 512], BF16)
    vs = sbt("vs", [128, 2 * 512], BF16)
    qn = sbt("qn", [128, 2 * 512], BF16)
    kn = sbt("kn", [128, 2 * 512], BF16)
    kbt = sbt("kbt", [128, 2 * 512], BF16)
    vbt = sbt("vbt", [128, 2 * 512], BF16)
    sq = sbt("sqb", [128, 512], BF16)
    rows = sbt("rows", [1, 10 * 512])
    rowbf = sbt("rowbf", [1, 512], BF16)
    bA = sbt("bA", [128, 512], BF16)
    bC = sbt("bC", [128, 512], BF16)
    bT = sbt("bT", [128, 512], BF16)
    NCH = 4
    Y_sb = [sbt(f"Ysb{i}", [128, 512], BF16) for i in range(NCH)]
    atnT = [sbt(f"atnT{i}", [128, 128], BF16) for i in range(NCH)]
    ktok = [sbt(f"ktok{i}", [128, 256], BF16) for i in range(NCH)]
    wfm = [sbt(f"wfm{i}", [128, 256], BF16) for i in range(NCH)]
    Pb = sbt("Pb", [128, 128], BF16)
    PTb = sbt("PTb", [128, 128], BF16)
    Pb2 = sbt("Pb2", [128, 128], BF16)
    PTb2 = sbt("PTb2", [128, 128], BF16)
    S_sb = sbt("S_sb", [128, 512], BF16)
    UN = sbt("UN", [128, 256], BF16)
    ogrp = sbt("ogrp", [128, 2 * 512], BF16)
    o2g = sbt("o2g", [128, 2 * 512], BF16)
    o3g = sbt("o3g", [128, 2 * 512], BF16)
    yg = sbt("yg", [128, 8 * 512])

    psS = pst("psS", [128, 512])
    psY = pst("psY", [128, 512])
    psW = pst("psW", [128, 512])
    psO = pst("psO", [128, 256])
    psB1 = pst("psB1", [128, 128])
    psB2 = pst("psB2", [128, 128])
    psT = pst("psT", [128, 512], BF16)
    psP = pst("psP", [128, 512])

    sems = {}
    for s in ("ld", "gld", "mmd", "evd", "dvd", "sto", "bar"):
        sems[s] = Sem(es.enter_context(nc.semaphore(s)), s)
    ld, gld, mmd, evd, dvd, sto, bar = (
        sems[k] for k in ("ld", "gld", "mmd", "evd", "dvd", "sto", "bar"))

    block = es.enter_context(nc.Block())
    E = Emit(nc)
    E.enable_vchain(dvd)

    R = lambda i: rows[:, i * 512:(i + 1) * 512]

    def mm_end(fn, waits=()):
        return E.op("tensor", fn, waits=waits, inc=(mmd, 1))

    for t_, src in [(nwc, nw_d), (mtril, mtril_d), (mtriu, mtriu_d), (mcaus, mcaus_d)]:
        E.op("sync", lambda s, t_=t_, src=src: s.dma_start(out=t_[:], in_=src), inc=(ld, 16))
    for t_, src, m in [(wq, wq_d, 2), (wk, wk_d, 2), (wv, wv_d, 2), (wo, wo_d, 8)]:
        E.op("gpsimd", lambda g, t_=t_, src=src, m=m: g.dma_start(
            out=t_[:].rearrange("p (m k) -> p m k", m=m),
            in_=src.rearrange("m p k -> p m k")), inc=(gld, 16))
    for t_, src in [(wb, wb_d), (wmix, wmix_d), (cdq, cq_d), (cdk, ck_d), (cdv, cv_d),
                    (eyeb, eye_d)]:
        E.op("gpsimd", lambda g, t_=t_, src=src: g.dma_start(out=t_[:], in_=src), inc=(gld, 16))
    for t_ in (ones_c, ones_r):
        E.op("vector", lambda v, t_=t_: v.memset(t_[:], 1.0), inc=(dvd, 1))
    for t_ in (S_sb, qpre, kpre, vpre):
        E.op("vector", lambda v, t_=t_: v.memset(t_[:], 0.0), inc=(dvd, 1))
    for e in ("tensor", "scalar", "vector"):
        E.wait(e, ld, ld.n)
        E.wait(e, gld, gld.n)
        E.wait(e, dvd, dvd.n)
    E.barrier(bar)

    for g in range(ngroups):
        E.op("sync", lambda s, g=g: s.dma_start(
            out=xg[:].rearrange("p (k t) -> p k t", k=8),
            in_=x_d.rearrange("p (k t) -> p k t", k=8)[:, :, g * 512:(g + 1) * 512]),
            inc=(ld, 16))
        E.op("gpsimd", lambda gp, g=g: gp.dma_start(
            out=xb[:].rearrange("p (k t) -> p k t", k=8),
            in_=x_d.rearrange("p (k t) -> p k t", k=8)[:, :, g * 512:(g + 1) * 512]),
            inc=(gld, 16))
        ld_x = ld.n
        gld_x = gld.n
        E.wait("tensor", ld, ld_x)
        E.wait("tensor", gld, gld_x)
        E.wait("scalar", ld, ld_x)
        E.wait("scalar", gld, gld_x)
        for k in range(8):
            E.op("tensor", lambda t, k=k: t.matmul(
                psP[0:1, :], ones_c[:], xb[:, k * 512:(k + 1) * 512],
                start=(k == 0), stop=(k == 7), skip_group_check=True))
        for k in range(8):
            E.op("scalar", lambda s, k=k: s.activation(
                sq[:], xb[:, k * 512:(k + 1) * 512], AF.Square),
                waits=[(mmd, mmd.n)] if k > 0 else [], inc=(evd, 1))
            mm_end(lambda t, k=k: t.matmul(
                psW[0:1, 0:512], ones_c[:], sq[:],
                start=(k == 0), stop=(k == 7), skip_group_check=True),
                waits=[(evd, evd.n)])
        stats_mm = mmd.n
        E.op("vector", lambda v: v.tensor_scalar_mul(R(2), psP[0:1, :], 1.0 / 1024),
             waits=[(mmd, stats_mm)])
        E.op("vector", lambda v: v.tensor_scalar_mul(R(3), psW[0:1, 0:512], 1.0 / 1024))
        E.op("vector", lambda v: v.scalar_tensor_tensor(R(9), R(2), -1.0, R(2), ALU.mult, ALU.mult))
        E.op("vector", lambda v: v.tensor_add(R(3), R(3), R(9)))
        E.op("vector", lambda v: v.tensor_scalar_add(R(3), R(3), 1e-5), inc=(dvd, 1))
        E.op("scalar", lambda s: s.activation(R(9), R(3), AF.Sqrt),
             waits=[(dvd, dvd.n)], inc=(evd, 1))
        E.op("vector", lambda v: v.reciprocal(R(3), R(9)), waits=[(evd, evd.n)])
        E.op("vector", lambda v: v.scalar_tensor_tensor(R(9), R(2), -1.0, R(3), ALU.mult, ALU.mult))
        E.op("vector", lambda v: v.tensor_copy(rowbf[:], R(3)), inc=(dvd, 1))
        mm_end(lambda t: t.matmul(psP[:, :], ones_r[:], rowbf[:], start=True, stop=True),
               waits=[(dvd, dvd.n)])
        E.op("scalar", lambda s: s.activation(bA[:], psP[:, :], AF.Copy),
             waits=[(mmd, mmd.n)], inc=(evd, 1))
        ev_bA = evd.n
        E.op("vector", lambda v: v.tensor_copy(rowbf[:], R(9)),
             waits=[(mmd, mmd.n)], inc=(dvd, 1))
        mm_end(lambda t: t.matmul(psP[:, :], ones_r[:], rowbf[:], start=True, stop=True),
               waits=[(dvd, dvd.n), (evd, ev_bA)])
        E.op("scalar", lambda s: s.activation(bC[:], psP[:, :], AF.Copy),
             waits=[(mmd, mmd.n)], inc=(evd, 1))
        ev_bc = evd.n
        for k in range(8):
            E.op("vector", lambda v, k=k: v.tensor_mul(
                nx[:, k * 512:(k + 1) * 512], xb[:, k * 512:(k + 1) * 512], bA[:]),
                waits=[(evd, ev_bc)] if k == 0 else [])
            E.op("vector", lambda v, k=k: v.tensor_add(
                nx[:, k * 512:(k + 1) * 512], nx[:, k * 512:(k + 1) * 512], bC[:]),
                inc=(dvd, 1) if k == 7 else None)
        E.wait("tensor", dvd, dvd.n)

        for k in range(8):
            E.op("tensor", lambda t, k=k: t.matmul(
                psP[0:1, :], wb[:, k:k + 1], nx[:, k * 512:(k + 1) * 512],
                start=(k == 0), stop=(k == 7), skip_group_check=True))
            E.op("tensor", lambda t, k=k: t.matmul(
                psW[0:1, 0:512], wmix[:, k:k + 1], nx[:, k * 512:(k + 1) * 512],
                start=(k == 0), stop=(k == 7), skip_group_check=True),
                **({"inc": (mmd, 1)} if k == 7 else {}))
        E.op("scalar", lambda s: s.activation(R(4), psP[0:1, :], AF.Sigmoid),
             waits=[(mmd, mmd.n)])
        E.op("scalar", lambda s: s.activation(R(5), psW[0:1, 0:512], AF.Sigmoid), inc=(evd, 1))
        ev_bm = evd.n
        E.wait("tensor", evd, ev_bm)

        for (wt, pre, cd, dst) in ((wq, qpre, cdq, qs), (wk, kpre, cdk, ks), (wv, vpre, cdv, vs)):
            E.op("vector", lambda v, pre=pre: v.tensor_copy(
                pre[:].rearrange("p (m w) -> p m w", m=2)[:, :, 0:3],
                pre[:].rearrange("p (m w) -> p m w", m=2)[:, :, 512:515]), inc=(dvd, 1))
            E.wait("tensor", dvd, dvd.n)
            E.wait("scalar", dvd, dvd.n)
            for m in range(2):
                for k in range(8):
                    E.op("tensor", lambda t, wt=wt, m=m, k=k: t.matmul(
                        psP[:, 0:512], wt[:, m * 1024 + k * 128: m * 1024 + (k + 1) * 128],
                        nx[:, k * 512:(k + 1) * 512], start=(k == 0), stop=(k == 7)),
                        **({"inc": (mmd, 1)} if k == 7 else {}))
                E.op("scalar", lambda s, pre=pre, m=m: s.activation(
                    pre[:, m * 515 + 3: m * 515 + 515], psP[:, 0:512], AF.Copy),
                    waits=[(mmd, mmd.n)], inc=(evd, 1))
                E.wait("tensor", evd, evd.n)
                for sft in range(4):
                    E.op("tensor", lambda t, cd=cd, pre=pre, m=m, sft=sft: t.matmul(
                        psP[:, 0:512], cd[:, (sft * 2 + m) * 128:(sft * 2 + m + 1) * 128],
                        pre[:, m * 515 + sft: m * 515 + sft + 512],
                        start=(sft == 0), stop=(sft == 3)),
                        **({"inc": (mmd, 1)} if sft == 3 else {}))
                E.op("scalar", lambda s, dst=dst, m=m: s.activation(
                    dst[:, m * 512:(m + 1) * 512], psP[:, 0:512], AF.Silu),
                    waits=[(mmd, mmd.n)], inc=(evd, 1))
                E.wait("tensor", evd, evd.n)
        for (src, slot) in ((qs, 6), (ks, 7)):
            for m in range(2):
                E.op("scalar", lambda s, src=src, m=m: s.activation(
                    sq[:], src[:, m * 512:(m + 1) * 512], AF.Square),
                    waits=[(mmd, mmd.n)], inc=(evd, 1))
                mm_end(lambda t, m=m: t.matmul(
                    psP[0:1, :], ones_c[:], sq[:],
                    start=(m == 0), stop=(m == 1), skip_group_check=True),
                    waits=[(evd, evd.n)])
            E.op("vector", lambda v, slot=slot: v.tensor_scalar_add(R(slot), psP[0:1, :], 1e-6),
                 waits=[(mmd, mmd.n)], inc=(dvd, 1))
            E.op("scalar", lambda s, slot=slot: s.activation(R(9), R(slot), AF.Sqrt),
                 waits=[(dvd, dvd.n)], inc=(evd, 1))
            E.op("vector", lambda v, slot=slot: v.reciprocal(R(slot), R(9)),
                 waits=[(evd, evd.n)], inc=(dvd, 1))
        E.op("vector", lambda v: v.tensor_mul(R(8), R(7), R(4)), inc=(dvd, 1))

        def bcast_apply(row_slot, dst, src):
            E.op("vector", lambda v, row_slot=row_slot: v.tensor_copy(rowbf[:], R(row_slot)),
                 waits=[(dvd, dvd.n), (mmd, mmd.n)], inc=(dvd, 1))
            mm_end(lambda t: t.matmul(psP[:, :], ones_r[:], rowbf[:], start=True, stop=True),
                   waits=[(dvd, dvd.n)])
            E.op("scalar", lambda s: s.activation(bT[:], psP[:, :], AF.Copy),
                 waits=[(mmd, mmd.n)], inc=(evd, 1))
            ev = evd.n
            for m in range(2):
                E.op("vector", lambda v, m=m, dst=dst, src=src: v.tensor_mul(
                    dst[:, m * 512:(m + 1) * 512], src[:, m * 512:(m + 1) * 512], bT[:]),
                    waits=[(evd, ev)] if m == 0 else [],
                    inc=(dvd, 1) if m == 1 else None)
            E.wait("tensor", dvd, dvd.n)
        bcast_apply(6, qn, qs)
        bcast_apply(7, kn, ks)
        bcast_apply(8, kbt, ks)
        bcast_apply(4, vbt, vs)
        E.barrier(bar)
        if debug and g == 0:
            for nm, t_ in [("d_xb", xb), ("d_bA", bA), ("d_bC", bC),
                           ("d_nx", nx), ("d_qs", qs), ("d_ks", ks), ("d_vs", vs),
                           ("d_qn", qn), ("d_kn", kn), ("d_kbt", kbt), ("d_vbt", vbt)]:
                E.op("sync", lambda s, nm=nm, t_=t_: s.dma_start(out=dbg[nm], in_=t_[:]),
                     inc=(sto, 16))
            E.op("sync", lambda s: s.dma_start(out=dbg["d_rows"], in_=rows[:]), inc=(sto, 16))
            E.barrier(bar)

        for c in range(NCH):
            def CS(buf, m, c=c):
                return buf[:, m * 512 + c * 128: m * 512 + (c + 1) * 128]
            for m in range(2):
                E.op("tensor", lambda t, m=m, CS=CS: t.matmul(
                    psB1[:], CS(kbt, m), CS(kn, m), start=(m == 0), stop=(m == 1)),
                    **({"inc": (mmd, 1)} if m == 1 else {}))
            E.op("vector", lambda v: v.tensor_mul(Pb[:], psB1[:], mtril[:]),
                 waits=[(mmd, mmd.n)], inc=(dvd, 1))
            dv_A = dvd.n
            for m in range(2):
                E.op("tensor", lambda t, m=m, CS=CS: t.matmul(
                    psB2[:], CS(kn, m), CS(kbt, m), start=(m == 0), stop=(m == 1)),
                    **({"inc": (mmd, 1)} if m == 1 else {}))
            E.op("vector", lambda v: v.tensor_mul(PTb[:], psB2[:], mtriu[:]),
                 waits=[(mmd, mmd.n)], inc=(dvd, 1))
            dv_AT = dvd.n
            E.wait("tensor", dvd, dv_A)
            for m in range(2):
                E.op("tensor", lambda t, m=m, CS=CS: t.matmul(
                    psB1[:], CS(kn, m), CS(qn, m), start=(m == 0), stop=(m == 1)),
                    **({"inc": (mmd, 1)} if m == 1 else {}))
            E.op("vector", lambda v, c=c: v.tensor_mul(atnT[c][:], psB1[:], mcaus[:]),
                 waits=[(mmd, mmd.n)], inc=(dvd, 1))
            E.wait("tensor", evd, evd.n)
            for m in range(2):
                E.op("tensor", lambda t, m=m, CS=CS: t.transpose(
                    psT[:, m * 128:(m + 1) * 128], CS(vbt, m), eyeb[:]))
            for m in range(2):
                E.op("tensor", lambda t, m=m, CS=CS: t.transpose(
                    psT[:, 256 + m * 128: 256 + (m + 1) * 128], CS(kbt, m), eyeb[:]),
                    **({"inc": (mmd, 1)} if m == 1 else {}))
            E.op("scalar", lambda s, c=c: s.activation(Y_sb[c][:], psT[:], AF.Copy),
                 waits=[(mmd, mmd.n)], inc=(evd, 1))
            ev_y0 = evd.n
            E.wait("tensor", evd, ev_y0)
            for m in range(2):
                E.op("tensor", lambda t, m=m, CS=CS: t.transpose(
                    psT[:, m * 128:(m + 1) * 128], CS(kn, m), eyeb[:]),
                    **({"inc": (mmd, 1)} if m == 1 else {}))
            E.op("scalar", lambda s, c=c: s.activation(ktok[c][:], psT[:, 0:256], AF.Copy),
                 waits=[(mmd, mmd.n)], inc=(evd, 1))
            ev_kt = evd.n
            E.wait("tensor", dvd, dvd.n)
            E.op("tensor", lambda t, c=c: t.matmul(psY[:], eyeb[:], Y_sb[c][:],
                                                   start=True, stop=True))
            P_cur, PT_cur = Pb, PTb
            P_nxt, PT_nxt = Pb2, PTb2
            for lev in range(7):
                mm_end(lambda t, c=c, PT_cur=PT_cur, lev=lev: t.matmul(
                    psY[:], PT_cur[:], Y_sb[c][:], start=False, stop=(lev == 6),
                    skip_group_check=True))
                E.op("scalar", lambda s, c=c: s.activation(Y_sb[c][:], psY[:], AF.Copy),
                     waits=[(mmd, mmd.n)], inc=(evd, 1))
                ev_y = evd.n
                if lev < 6:
                    E.op("tensor", lambda t, P_cur=P_cur, PT_cur=PT_cur: t.matmul(
                        psB1[:], PT_cur[:], P_cur[:], start=True, stop=True))
                    mm_end(lambda t, P_cur=P_cur, PT_cur=PT_cur: t.matmul(
                        psB2[:], P_cur[:], PT_cur[:], start=True, stop=True))
                    E.op("vector", lambda v, P_nxt=P_nxt: v.tensor_copy(P_nxt[:], psB1[:]),
                         waits=[(mmd, mmd.n)])
                    E.op("vector", lambda v, PT_nxt=PT_nxt: v.tensor_copy(PT_nxt[:], psB2[:]),
                         inc=(dvd, 1))
                    E.wait("tensor", dvd, dvd.n)
                    E.wait("tensor", evd, ev_y)
                    P_cur, P_nxt = P_nxt, P_cur
                    PT_cur, PT_nxt = PT_nxt, PT_cur
                else:
                    E.wait("tensor", evd, ev_y)
            for m in range(2):
                E.op("tensor", lambda t, m=m, c=c: t.transpose(
                    psT[:, m * 128:(m + 1) * 128],
                    Y_sb[c][:, 256 + m * 128: 256 + (m + 1) * 128], eyeb[:]),
                    **({"inc": (mmd, 1)} if m == 1 else {}))
            E.op("scalar", lambda s, c=c: s.activation(wfm[c][:], psT[:, 0:256], AF.Copy),
                 waits=[(mmd, mmd.n)], inc=(evd, 1))
        E.barrier(bar)
        if debug and g == 0:
            for nm, t_ in [("d_Y0", Y_sb[0]), ("d_atnT", atnT[0]), ("d_ktok", ktok[0]),
                           ("d_wfm", wfm[0])]:
                E.op("sync", lambda s, nm=nm, t_=t_: s.dma_start(out=dbg[nm], in_=t_[:]),
                     inc=(sto, 16))
            E.barrier(bar)

        for c in range(NCH):
            first = (g == 0 and c == 0)
            if not first:
                E.op("scalar", lambda s: s.activation(S_sb[:], psS[:], AF.Copy),
                     waits=[(mmd, mmd.n)], inc=(evd, 1))
                E.wait("tensor", evd, evd.n)
            for m in range(2):
                E.op("tensor", lambda t, c=c, m=m: t.matmul(
                    psW[:, 0:256], wfm[c][:, m * 128:(m + 1) * 128], S_sb[:, m * 256:(m + 1) * 256],
                    start=(m == 0), stop=(m == 1)),
                    **({"inc": (mmd, 1)} if m == 1 else {}))
            E.op("vector", lambda v, c=c: v.tensor_sub(UN[:], Y_sb[c][:, 0:256], psW[:, 0:256]),
                 waits=[(mmd, mmd.n)], inc=(dvd, 1))
            E.wait("tensor", dvd, dvd.n)
            for fm in range(2):
                for m in range(2):
                    E.op("tensor", lambda t, c=c, fm=fm, m=m: t.matmul(
                        psO[:, fm * 128:(fm + 1) * 128],
                        S_sb[:, m * 256 + fm * 128: m * 256 + (fm + 1) * 128],
                        qn[:, m * 512 + c * 128: m * 512 + (c + 1) * 128],
                        start=(m == 0), stop=False, skip_group_check=True))
                E.op("tensor", lambda t, c=c, fm=fm: t.matmul(
                    psO[:, fm * 128:(fm + 1) * 128], UN[:, fm * 128:(fm + 1) * 128],
                    atnT[c][:], start=False, stop=True, skip_group_check=True),
                    **({"inc": (mmd, 1)} if fm == 1 else {}))
            o_mm = mmd.n
            for m in range(2):
                E.op("tensor", lambda t, c=c, m=m, first=first: t.matmul(
                    psS[:, m * 256:(m + 1) * 256], ktok[c][:, m * 128:(m + 1) * 128],
                    UN[:], start=(first and m == 0), stop=False, skip_group_check=True),
                    **({"inc": (mmd, 1)} if m == 1 else {}))
            E.op("scalar", lambda s, c=c, o_mm=o_mm: s.activation(
                ogrp[:].rearrange("p (m w) -> p m w", m=2)[:, :, c * 128:(c + 1) * 128],
                psO[:].rearrange("p (m w) -> p m w", m=2), AF.Copy),
                waits=[(mmd, o_mm)], inc=(evd, 1))
        E.wait("vector", evd, evd.n)
        E.op("vector", lambda v: v.tensor_copy(rowbf[:], R(5)),
             waits=[(mmd, mmd.n)], inc=(dvd, 1))
        mm_end(lambda t: t.matmul(psP[:, :], ones_r[:], rowbf[:], start=True, stop=True),
               waits=[(dvd, dvd.n)])
        E.op("scalar", lambda s: s.activation(bT[:], psP[:, :], AF.Copy),
             waits=[(mmd, mmd.n)], inc=(evd, 1))
        ev_mx = evd.n
        for m in range(2):
            E.op("vector", lambda v, m=m: v.tensor_sub(
                o2g[:, m * 512:(m + 1) * 512], ogrp[:, m * 512:(m + 1) * 512],
                vs[:, m * 512:(m + 1) * 512]), waits=[(evd, ev_mx)] if m == 0 else [])
            E.op("vector", lambda v, m=m: v.tensor_mul(
                o2g[:, m * 512:(m + 1) * 512], o2g[:, m * 512:(m + 1) * 512], bT[:]))
            E.op("vector", lambda v, m=m: v.tensor_add(
                o2g[:, m * 512:(m + 1) * 512], o2g[:, m * 512:(m + 1) * 512],
                vs[:, m * 512:(m + 1) * 512]), inc=(dvd, 1) if m == 1 else None)
        for m in range(2):
            E.op("scalar", lambda s, m=m: s.activation(
                sq[:], o2g[:, m * 512:(m + 1) * 512], AF.Square),
                waits=[(dvd, dvd.n), (mmd, mmd.n)], inc=(evd, 1))
            mm_end(lambda t, m=m: t.matmul(
                psW[0:1, 0:512], ones_c[:], sq[:], start=(m == 0), stop=(m == 1),
                skip_group_check=True), waits=[(evd, evd.n)])
        E.op("vector", lambda v: v.tensor_scalar_mul(R(9), psW[0:1, 0:512], 1.0 / 256),
             waits=[(mmd, mmd.n)])
        E.op("vector", lambda v: v.tensor_scalar_add(R(9), R(9), 1e-5), inc=(dvd, 1))
        E.op("scalar", lambda s: s.activation(R(2), R(9), AF.Sqrt),
             waits=[(dvd, dvd.n)], inc=(evd, 1))
        E.op("vector", lambda v: v.reciprocal(R(9), R(2)), waits=[(evd, evd.n)])
        E.op("vector", lambda v: v.tensor_copy(rowbf[:], R(9)),
             waits=[(mmd, mmd.n)], inc=(dvd, 1))
        mm_end(lambda t: t.matmul(psP[:, :], ones_r[:], rowbf[:], start=True, stop=True),
               waits=[(dvd, dvd.n)])
        E.op("scalar", lambda s: s.activation(bT[:], psP[:, :], AF.Copy),
             waits=[(mmd, mmd.n)], inc=(evd, 1))
        ev_rr = evd.n
        for m in range(2):
            E.op("vector", lambda v, m=m: v.tensor_mul(
                o3g[:, m * 512:(m + 1) * 512], o2g[:, m * 512:(m + 1) * 512], bT[:]),
                waits=[(evd, ev_rr)] if m == 0 else [], inc=(dvd, 1))
            E.op("scalar", lambda s, m=m: s.activation(
                o3g[:, m * 512:(m + 1) * 512], o3g[:, m * 512:(m + 1) * 512],
                AF.Copy, scale=nwc[:, m:m + 1]),
                waits=[(dvd, dvd.n)], inc=(evd, 1))
        E.wait("tensor", evd, evd.n)
        for mp in range(8):
            for k in range(2):
                E.op("tensor", lambda t, mp=mp, k=k: t.matmul(
                    psP[:, 0:512], wo[:, mp * 256 + k * 128: mp * 256 + (k + 1) * 128],
                    o3g[:, k * 512:(k + 1) * 512], start=(k == 0), stop=(k == 1)),
                    **({"inc": (mmd, 1)} if k == 1 else {}))
            E.op("scalar", lambda s, mp=mp: s.activation(
                yg[:, mp * 512:(mp + 1) * 512], psP[:, 0:512], AF.Copy),
                waits=[(mmd, mmd.n)], inc=(evd, 1))
            E.wait("tensor", evd, evd.n)
        E.op("sync", lambda s, g=g: s.dma_start(
            out=y_d.rearrange("p (m t) -> p m t", m=8)[:, :, g * 512:(g + 1) * 512],
            in_=yg[:].rearrange("p (m t) -> p m t", m=8)),
            waits=[(evd, evd.n)], inc=(sto, 16))
        if debug and g == 0:
            for nm, t_ in [("d_ogrp", ogrp), ("d_o3g", o3g)]:
                E.op("sync", lambda s, nm=nm, t_=t_: s.dma_start(out=dbg[nm], in_=t_[:]),
                     inc=(sto, 16))
        E.barrier(bar)

    E.op("sync", lambda s: s.wait_ge(sto.h, sto.n))
    E.replay(block)
    es.close()
    return nc


def stage_L1_inputs(z, b, h, ngroups=16):
    x = np.asarray(z["L0_attn_a_hidden_states"][b], np.float32).T
    Wq = np.asarray(z["L0_attn_a_Wq"], np.float32)
    Wk = np.asarray(z["L0_attn_a_Wk"], np.float32)
    Wv = np.asarray(z["L0_attn_a_Wv"], np.float32)
    Wo = np.asarray(z["L0_attn_a_Wo"], np.float32)
    sl = slice(256 * h, 256 * (h + 1))
    return {
        "x": fm_layout(x),
        "wq": col_layout(Wq.T[:, sl]),
        "wk": col_layout(Wk.T[:, sl]),
        "wv": col_layout(Wv.T[:, sl]),
        "wo": col_layout(np.ascontiguousarray(Wo.T[sl, :])),
        "wb": vec_cols(np.asarray(z["L0_attn_a_Wb"], np.float32)[h]),
        "wmix": vec_cols(np.asarray(z["L0_attn_a_Wmix"], np.float32)[h]),
        "cq": diag_stage(np.asarray(z["L0_attn_a_conv_q"], np.float32)[sl]),
        "ck": diag_stage(np.asarray(z["L0_attn_a_conv_k"], np.float32)[sl]),
        "cv": diag_stage(np.asarray(z["L0_attn_a_conv_v"], np.float32)[sl]),
        "nw": np.ascontiguousarray(
            np.asarray(z["L0_attn_a_norm_w"], np.float32).reshape(2, 128).T),
        "mtril": MASK_TRILNEG, "mtriu": MASK_TRIUNEG, "mcaus": MASK_CAUS_T,
        "eye": EYE128,
    }


def kernel(**inputs):
    raise NotImplementedError("under construction: L2-L4 pending")


def build_ffn(add_ln_of_base=False, mamba_mode=False, Ttok=2048, groups=(512, 512, 512, 512), nexp=8, debug=False, stage=99):
    T = Ttok
    assert sum(groups) == T
    nc = bass.Bass("TRN2", target_bir_lowering=False, detect_race_conditions=False)

    def dram(name, shape, dt=F32, kind="ExternalInput"):
        return nc.dram_tensor(name, shape, dt, kind=kind).ap()

    xb_d = dram("xbase", [128, 8 * T])
    p_d = [dram(f"part{i}", [128, 8 * T]) for i in range(4)]
    if mamba_mode:
        ysq_d = [dram(f"ysq{i}", [1, T]) for i in range(4)]
    up_d = [dram(f"upW{e}", [32, 128, 1024]) for e in range(nexp)]
    dn_d = [dram(f"dnW{e}", [8, 128, 4096]) for e in range(nexp)]
    s0_d = [dram(f"s0W{i}", [32, 128, 4096]) for i in range(2)]
    s1a_d = [dram(f"s1aW{i}", [16, 128, 4096]) for i in range(2)]
    s1b_d = [dram(f"s1bW{i}", [32, 128, 2048]) for i in range(2)]
    s2_d = [dram(f"s2W{i}", [32, 128, 4096]) for i in range(2)]
    upb_d = dram("upb", [128, 8 * 32])
    dnb_d = dram("dnb", [128, 8 * 8])
    s0b_d = dram("s0b", [128, 2 * 32])
    s1ab_d = dram("s1ab", [128, 2 * 16])
    s1bb_d = dram("s1bb", [128, 2 * 32])
    s2b_d = dram("s2b", [128, 2 * 32])
    l0g_d = dram("l0g", [128, 2 * 32])
    l0b_d = dram("l0b", [128, 2 * 32])
    l1g_d = dram("l1g", [128, 2 * 32])
    l1b_d = dram("l1b", [128, 2 * 32])
    lng_d = dram("lng", [128, 8])
    lnb_d = dram("lnb", [128, 8])
    rw_d = dram("rw", [128, 64])
    oneh_d = dram("oneh", [8, 8 * 128])
    eye_d = dram("eyef", [128, 128])
    xo_d = dram("xout", [128, 8 * T], F32, kind="ExternalOutput")
    dbg = {}
    if debug:
        for nm, w, dt in [("d_mp", 8*512, F32), ("d_mrb", 512, BF16),
                          ("d_x1", 8*512, F32), ("d_nxl", 8*512, BF16),
                          ("d_wfr", 512, F32), ("d_A", 32*512, BF16),
                          ("d_B", 32*512, BF16), ("d_xacc", 8*512, F32),
                          ("d_prb", 64, F32), ("d_pcol", 8, F32), ("d_wbc", 512, BF16)]:
            dbg[nm] = nc.dram_tensor(nm, [128, w] if nm != "d_wfr" else [8, 512],
                                     dt, kind="ExternalOutput").ap()
        dbg["d_msq"] = nc.dram_tensor("d_msq", [1, 512], F32, kind="ExternalOutput").ap()

    es = ExitStack()
    sbt = lambda n_, s_, dt=F32: es.enter_context(nc.sbuf_tensor(n_, s_, dt))
    pst = lambda n_, s_, dt=F32: es.enter_context(nc.psum_tensor(n_, s_, dt))

    GM = max(groups)
    Ab = sbt("Ab", [128, 32 * GM], BF16)
    Bb = sbt("Bb", [128, 32 * GM], BF16)
    x1g = sbt("x1g", [128, 8 * GM])
    xbb = sbt("xbb", [128, 8 * GM], BF16)
    nxl = sbt("nxl", [128, 8 * GM], BF16)
    NWBUF = 2
    wbuf = [sbt(f"wbuf{i}", [128, 4096], BF16) for i in range(NWBUF)]
    fstage = [sbt(f"fstage{i}", [128, 1024]) for i in range(2)]
    upb = sbt("upb_s", [128, 8 * 32])
    dnb = sbt("dnb_s", [128, 8 * 8])
    s0b = sbt("s0b_s", [128, 2 * 32])
    s1ab = sbt("s1ab_s", [128, 2 * 16])
    s1bb = sbt("s1bb_s", [128, 2 * 32])
    s2b = sbt("s2b_s", [128, 2 * 32])
    l0g = sbt("l0g_s", [128, 2 * 32])
    l0b = sbt("l0b_s", [128, 2 * 32])
    l1g = sbt("l1g_s", [128, 2 * 32])
    l1b = sbt("l1b_s", [128, 2 * 32])
    lng = sbt("lng_s", [128, 8])
    lnb = sbt("lnb_s", [128, 8])
    rw = sbt("rw_s", [128, 64], BF16)
    eyef = sbt("eyef_s", [128, 128])
    ones_c = sbt("ones_c", [128, 1], BF16)
    ones_r = sbt("ones_r", [1, 128], BF16)
    sq = sbt("sqb", [128, GM], BF16)
    tchk = sbt("tchk", [128, GM], BF16)
    rows = sbt("rows", [1, 6 * GM])
    rowbf = sbt("rowbf", [1, GM], BF16)
    bA = sbt("bA", [128, GM], BF16)
    bC = sbt("bC", [128, GM], BF16)
    bA2 = sbt("bA2", [128, GM], BF16)
    bC2 = sbt("bC2", [128, GM], BF16)
    wbc = sbt("wbc", [128, GM], BF16)
    if mamba_mode:
        mp = sbt("mp", [128, 8 * GM])
        msq = sbt("msq", [1, GM])
        mrb = sbt("mrb", [128, GM], BF16)
    prb = sbt("prb", [128, 8 * 8])
    pcol = sbt("pcol", [128, 8])
    wfr = sbt("wfr", [8, GM], BF16)
    oneh = sbt("oneh_s", [8, 8 * 128], BF16)
    dtmp = sbt("dtmp", [128, 512])

    psA = pst("psA", [128, 512]); psB = pst("psB", [128, 512])
    psC = pst("psC", [128, 512]); psD = pst("psD", [128, 512])
    psR = pst("psR", [128, 512])
    psQ = pst("psQ", [128, 512])
    PAIRS = [(psA, psB), (psC, psD)]

    sems = {}
    for s in ("ld", "gld", "wld", "wfr_", "mmd", "evd", "dvd", "sto", "bar", "fld",
              "fcast", "wcd"):
        sems[s] = Sem(es.enter_context(nc.semaphore(s)), s)
    ld, gld, wld, wfree, mmd, evd, dvd, sto, bar, fld, fcast, wcast_done = (
        sems[k] for k in ("ld", "gld", "wld", "wfr_", "mmd", "evd", "dvd", "sto", "bar",
                          "fld", "fcast", "wcd"))

    block = es.enter_context(nc.Block())
    E = Emit(nc)
    E.enable_vchain(dvd)

    R = lambda i: rows[:, i * GM:(i + 1) * GM]

    def mm_end(fn, waits=()):
        return E.op("tensor", fn, waits=waits, inc=(mmd, 1))

    for t_, src in [(upb, upb_d), (dnb, dnb_d), (s0b, s0b_d), (s1ab, s1ab_d),
                    (s1bb, s1bb_d), (s2b, s2b_d), (l0g, l0g_d), (l0b, l0b_d),
                    (l1g, l1g_d), (l1b, l1b_d), (lng, lng_d), (lnb, lnb_d),
                    (eyef, eye_d)]:
        E.op("sync", lambda s, t_=t_, src=src: s.dma_start(out=t_[:], in_=src), inc=(ld, 16))
    E.op("gpsimd", lambda g: g.dma_start(out=rw[:], in_=rw_d), inc=(gld, 16))
    E.op("gpsimd", lambda g: g.dma_start(out=oneh[:], in_=oneh_d), inc=(gld, 16))
    E.op("vector", lambda v: v.memset(ones_c[:], 1.0), inc=(dvd, 1))
    E.op("vector", lambda v: v.memset(ones_r[:], 1.0), inc=(dvd, 1))
    for e in ("tensor", "scalar", "vector"):
        E.wait(e, ld, ld.n)
        E.wait(e, gld, gld.n)
        E.wait(e, dvd, dvd.n)
    E.barrier(bar)

    gcol = [0]
    colcnt = [0]
    sc_end = {}

    def stream_cols(src_d, mc, kwidth):
        cols = []
        for j in range(mc):
            gi = gcol[0]
            buf = gi % NWBUF
            waits = [(wfree, sc_end[gi - NWBUF])] if gi >= NWBUF else []
            if kwidth <= 1024:
                fb = fcast.n % 2
                ci = fcast.n
                E.op("sync", lambda s_, j=j, fb=fb, src_d=src_d, kwidth=kwidth:
                     s_.dma_start(out=fstage[fb][:, 0:kwidth], in_=src_d[j]),
                     waits=[(wcast_done, ci - 1)] if ci >= 2 else [],
                     inc=(fld, 16))
                def cast(v_, fb=fb, buf=buf, kwidth=kwidth):
                    return v_.tensor_copy(wbuf[buf][:, 0:kwidth], fstage[fb][:, 0:kwidth])
                E.op("vector", cast, waits=[(fld, fld.n)] + waits, inc=(wcast_done, 1))
                fcast.n += 1
                E._last_v = (wcast_done, wcast_done.n)
                cols.append((buf, 0, (wcast_done, wcast_done.n)))
                colcnt[0] += 1
                sc_end[gi] = colcnt[0]
                gcol[0] += 1
                continue
            else:
                E.op("gpsimd", lambda g, j=j, buf=buf, src_d=src_d, kwidth=kwidth:
                     g.dma_start(out=wbuf[buf][:, 0:kwidth], in_=src_d[j]),
                     waits=waits, inc=(wld, 16))
            cols.append((buf, 0, (wld, wld.n)))
            colcnt[0] += 1
            sc_end[gi] = colcnt[0]
            gcol[0] += 1
        return cols

    def matmul_phase(src_d, mc, kc, rhs_sb, rhs_stride, writer, G, pre_waits=()):
        cols = stream_cols(src_d, mc, kc * 128)
        evd_base = evd.n
        for pw in pre_waits:
            E.wait("tensor", pw[0], pw[1])
        tile_list = []
        t0 = 0
        while t0 < G:
            tw = min(512, G - t0)
            tile_list.append((t0, tw))
            t0 += tw
        for m, (buf, off, need) in enumerate(cols):
            pair = PAIRS[m % 2]
            E.wait("tensor", need[0], need[1])
            if m >= 2:
                E.wait("tensor", evd, evd_base + m - 1)
            else:
                E.wait("tensor", evd, evd_base)
            for k in range(kc):
                for ti, (t0, tw) in enumerate(tile_list):
                    last = (k == kc - 1) and (ti == len(tile_list) - 1)
                    def mmf(t, buf=buf, off=off, k=k, t0=t0, tw=tw, pair=pair, ti=ti, kc=kc):
                        return t.matmul(pair[ti][:, 0:tw],
                                        wbuf[buf][:, off + k * 128: off + (k + 1) * 128],
                                        rhs_sb[:, k * rhs_stride + t0: k * rhs_stride + t0 + tw],
                                        start=(k == 0), stop=(k == kc - 1))
                    E.op("tensor", mmf, inc=(mmd, 1) if last else None)
            writer(m, [(pair[ti], t0, tw) for ti, (t0, tw) in enumerate(tile_list)], mmd.n)

    def act_writer(dst_sb, bias_sb, bias_off, func, G, scale_sb=None, scale_off=0):
        def writer(m, tiles, mmc):
            for ti, (ps, t0, tw) in enumerate(tiles):
                def f(s, m=m, ps=ps, t0=t0, tw=tw):
                    kw = dict(bias=bias_sb[:, bias_off + m: bias_off + m + 1])
                    if scale_sb is not None:
                        kw["scale"] = scale_sb[:, scale_off + m: scale_off + m + 1]
                    return s.activation(dst_sb[:, m * G + t0: m * G + t0 + tw],
                                        ps[:, 0:tw], func, **kw)
                E.op("scalar", f,
                     waits=[(mmd, mmc)] if ti == 0 else [],
                     inc=(wfree, 1) if ti == 0 else (evd, 1))
            if len(tiles) == 1:
                E.op("scalar", lambda s: s.engine_nop() if False else s.nop(), inc=(evd, 1))
        return writer

    def ln_inline(src_sb, dst_sb, G, gcol_sb, gcol_off, bcol_sb, bcol_off, mcchunks,
                  gelu):
        for m in range(mcchunks):
            E.op("scalar", lambda s, m=m: s.activation(sq[:, 0:G], src_sb[:, m * G:(m + 1) * G], AF.Square),
                 waits=[(mmd, mmd.n)], inc=(evd, 1))
            E.op("tensor", lambda t, m=m, G=G: t.matmul(
                psR[0:1, 0:G], ones_c[:], src_sb[:, m * G:(m + 1) * G],
                start=(m == 0), stop=(m == mcchunks - 1), skip_group_check=True),
                waits=[(evd, evd.n)])
            mm_end(lambda t, m=m, G=G: t.matmul(
                psQ[0:1, 0:G], ones_c[:], sq[:, 0:G],
                start=(m == 0), stop=(m == mcchunks - 1), skip_group_check=True))
        sm = mmd.n
        invf = 1.0 / (mcchunks * 128)
        E.op("vector", lambda v: v.tensor_scalar_mul(R(2)[:, 0:G], psR[0:1, 0:G], invf),
             waits=[(mmd, sm)])
        E.op("vector", lambda v: v.tensor_scalar_mul(R(3)[:, 0:G], psQ[0:1, 0:G], invf))
        E.op("vector", lambda v: v.scalar_tensor_tensor(R(4)[:, 0:G], R(2)[:, 0:G], -1.0,
                                                        R(2)[:, 0:G], ALU.mult, ALU.mult))
        E.op("vector", lambda v: v.tensor_add(R(3)[:, 0:G], R(3)[:, 0:G], R(4)[:, 0:G]))
        E.op("vector", lambda v: v.tensor_scalar_add(R(3)[:, 0:G], R(3)[:, 0:G], 1e-5),
             inc=(dvd, 1))
        E.op("scalar", lambda s: s.activation(R(4)[:, 0:G], R(3)[:, 0:G], AF.Sqrt),
             waits=[(dvd, dvd.n)], inc=(evd, 1))
        E.op("vector", lambda v: v.reciprocal(R(3)[:, 0:G], R(4)[:, 0:G]), waits=[(evd, evd.n)])
        E.op("vector", lambda v: v.scalar_tensor_tensor(R(4)[:, 0:G], R(2)[:, 0:G], -1.0,
                                                        R(3)[:, 0:G], ALU.mult, ALU.mult))
        E.op("vector", lambda v: v.tensor_copy(rowbf[:, 0:G], R(3)[:, 0:G]),
             waits=[(mmd, mmd.n)], inc=(dvd, 1))
        t0 = 0
        while t0 < G:
            tw = min(512, G - t0)
            mm_end(lambda t, t0=t0, tw=tw: t.matmul(psR[:, 0:tw], ones_r[:],
                   rowbf[:, t0:t0 + tw], start=True, stop=True),
                   waits=[(dvd, dvd.n)] if t0 == 0 else [])
            E.op("scalar", lambda s, t0=t0, tw=tw: s.activation(bA2[:, t0:t0 + tw],
                 psR[:, 0:tw], AF.Copy), waits=[(mmd, mmd.n)], inc=(evd, 1))
            t0 += tw
        E.op("vector", lambda v: v.tensor_copy(rowbf[:, 0:G], R(4)[:, 0:G]),
             waits=[(mmd, mmd.n), (evd, evd.n)], inc=(dvd, 1))
        t0 = 0
        while t0 < G:
            tw = min(512, G - t0)
            mm_end(lambda t, t0=t0, tw=tw: t.matmul(psR[:, 0:tw], ones_r[:],
                   rowbf[:, t0:t0 + tw], start=True, stop=True),
                   waits=[(dvd, dvd.n)] if t0 == 0 else [])
            E.op("scalar", lambda s, t0=t0, tw=tw: s.activation(bC2[:, t0:t0 + tw],
                 psR[:, 0:tw], AF.Copy), waits=[(mmd, mmd.n)], inc=(evd, 1))
            t0 += tw
        ev_bc = evd.n
        for m in range(mcchunks):
            E.op("vector", lambda v, m=m: v.tensor_mul(tchk[:, 0:G], src_sb[:, m * G:(m + 1) * G],
                 bA2[:, 0:G]), waits=[(evd, ev_bc)] if m == 0 else [(evd, evd.n)])
            E.op("vector", lambda v, m=m: v.tensor_add(tchk[:, 0:G], tchk[:, 0:G], bC2[:, 0:G]),
                 inc=(dvd, 1))
            E.op("scalar", lambda s, m=m: s.activation(
                dst_sb[:, m * G:(m + 1) * G], tchk[:, 0:G],
                AF.Gelu if gelu else AF.Identity,
                scale=gcol_sb[:, gcol_off + m: gcol_off + m + 1],
                bias=bcol_sb[:, bcol_off + m: bcol_off + m + 1]),
                waits=[(dvd, dvd.n)], inc=(evd, 1))
    toff = 0
    for gi, G in enumerate(groups):
        ntc = G // 128
        E.op("sync", lambda s, toff=toff, G=G: s.dma_start(
            out=x1g[:, 0:8 * G].rearrange("p (k t) -> p k t", k=8),
            in_=xb_d.rearrange("p (k t) -> p k t", k=8)[:, :, toff:toff + G]),
            inc=(ld, 16))
        if add_ln_of_base:
            E.op("gpsimd", lambda g, toff=toff, G=G: g.dma_start(
                out=xbb[:, 0:8 * G].rearrange("p (k t) -> p k t", k=8),
                in_=xb_d.rearrange("p (k t) -> p k t", k=8)[:, :, toff:toff + G]),
                inc=(gld, 16))
        E.wait("gpsimd", ld, ld.n)
        ptgt = mp if mamba_mode else x1g
        for i in range(4):
            if mamba_mode and i == 0:
                E.op("gpsimd", lambda g, toff=toff, G=G: g.dma_start(
                    out=mp[:, 0:8 * G].rearrange("p (k t) -> p k t", k=8),
                    in_=p_d[0].rearrange("p (k t) -> p k t", k=8)[:, :, toff:toff + G]),
                    inc=(gld, 16))
            else:
                E.op("gpsimd", lambda g, i=i, toff=toff, G=G, ptgt=ptgt: g.dma_start(
                    out=ptgt[:, 0:8 * G].rearrange("p (k t) -> p k t", k=8),
                    in_=p_d[i].rearrange("p (k t) -> p k t", k=8)[:, :, toff:toff + G],
                    accum_op=ALU.add, max_dma_last_dim=2048), inc=(gld, 16))
            E.wait("gpsimd", gld, gld.n)
        if mamba_mode:
            E.op("gpsimd", lambda g, toff=toff, G=G: g.dma_start(
                out=msq[:, 0:G], in_=ysq_d[0][:, toff:toff + G]), inc=(gld, 16))
            E.wait("gpsimd", gld, gld.n)
            for i in range(1, 4):
                E.op("gpsimd", lambda g, i=i, toff=toff, G=G: g.dma_start(
                    out=msq[:, 0:G], in_=ysq_d[i][:, toff:toff + G],
                    accum_op=ALU.add, max_dma_last_dim=2048), inc=(gld, 16))
                E.wait("gpsimd", gld, gld.n)
        ld_acc = gld.n
        if add_ln_of_base:
            E.wait("tensor", gld, ld_acc)
            E.wait("scalar", gld, ld_acc)
            for m in range(8):
                E.op("scalar", lambda s, m=m, G=G: s.activation(sq[:, 0:G],
                     xbb[:, m * G:(m + 1) * G], AF.Square),
                     waits=[(mmd, mmd.n)], inc=(evd, 1))
                E.op("tensor", lambda t, m=m, G=G: t.matmul(
                    psR[0:1, 0:G], ones_c[:], xbb[:, m * G:(m + 1) * G],
                    start=(m == 0), stop=(m == 7), skip_group_check=True),
                    waits=[(evd, evd.n)])
                mm_end(lambda t, m=m, G=G: t.matmul(
                    psQ[0:1, 0:G], ones_c[:], sq[:, 0:G],
                    start=(m == 0), stop=(m == 7), skip_group_check=True))
            sm = mmd.n
            E.op("vector", lambda v, G=G: v.tensor_scalar_mul(R(2)[:, 0:G], psR[0:1, 0:G], 1.0 / 1024),
                 waits=[(mmd, sm)])
            E.op("vector", lambda v, G=G: v.tensor_scalar_mul(R(3)[:, 0:G], psQ[0:1, 0:G], 1.0 / 1024))
            E.op("vector", lambda v, G=G: v.scalar_tensor_tensor(R(4)[:, 0:G], R(2)[:, 0:G], -1.0,
                                                                 R(2)[:, 0:G], ALU.mult, ALU.mult))
            E.op("vector", lambda v, G=G: v.tensor_add(R(3)[:, 0:G], R(3)[:, 0:G], R(4)[:, 0:G]))
            E.op("vector", lambda v, G=G: v.tensor_scalar_add(R(3)[:, 0:G], R(3)[:, 0:G], 1e-5),
                 inc=(dvd, 1))
            E.op("scalar", lambda s, G=G: s.activation(R(4)[:, 0:G], R(3)[:, 0:G], AF.Sqrt),
                 waits=[(dvd, dvd.n)], inc=(evd, 1))
            E.op("vector", lambda v, G=G: v.reciprocal(R(3)[:, 0:G], R(4)[:, 0:G]),
                 waits=[(evd, evd.n)])
            E.op("vector", lambda v, G=G: v.scalar_tensor_tensor(R(4)[:, 0:G], R(2)[:, 0:G], -1.0,
                                                                 R(3)[:, 0:G], ALU.mult, ALU.mult))
            E.op("vector", lambda v, G=G: v.tensor_copy(rowbf[:, 0:G], R(3)[:, 0:G]),
                 waits=[(mmd, mmd.n)], inc=(dvd, 1))
            t0 = 0
            while t0 < G:
                tw = min(512, G - t0)
                mm_end(lambda t, t0=t0, tw=tw: t.matmul(psR[:, 0:tw], ones_r[:],
                       rowbf[:, t0:t0 + tw], start=True, stop=True),
                       waits=[(dvd, dvd.n)] if t0 == 0 else [])
                E.op("scalar", lambda s, t0=t0, tw=tw: s.activation(bA[:, t0:t0 + tw],
                     psR[:, 0:tw], AF.Copy), waits=[(mmd, mmd.n)], inc=(evd, 1))
                t0 += tw
            E.op("vector", lambda v, G=G: v.tensor_copy(rowbf[:, 0:G], R(4)[:, 0:G]),
                 waits=[(mmd, mmd.n), (evd, evd.n)], inc=(dvd, 1))
            t0 = 0
            while t0 < G:
                tw = min(512, G - t0)
                mm_end(lambda t, t0=t0, tw=tw: t.matmul(psR[:, 0:tw], ones_r[:],
                       rowbf[:, t0:t0 + tw], start=True, stop=True),
                       waits=[(dvd, dvd.n)] if t0 == 0 else [])
                E.op("scalar", lambda s, t0=t0, tw=tw: s.activation(bC[:, t0:t0 + tw],
                     psR[:, 0:tw], AF.Copy), waits=[(mmd, mmd.n)], inc=(evd, 1))
                t0 += tw
            ev_bc0 = evd.n
            E.wait("vector", gld, ld_acc)
            E.wait("vector", ld, ld.n)
            E.wait("vector", evd, ev_bc0)
            for m in range(8):
                E.op("vector", lambda v, m=m, G=G: v.tensor_mul(tchk[:, 0:G],
                     xbb[:, m * G:(m + 1) * G], bA[:, 0:G]),
                     waits=[(dvd, dvd.n)], inc=(dvd, 1))
                E.op("vector", lambda v, m=m, G=G: v.tensor_add(tchk[:, 0:G], tchk[:, 0:G],
                     bC[:, 0:G]), waits=[(dvd, dvd.n)], inc=(dvd, 1))
                E.op("vector", lambda v, m=m, G=G: v.tensor_add(
                    x1g[:, m * G:(m + 1) * G], x1g[:, m * G:(m + 1) * G], tchk[:, 0:G]),
                    waits=[(dvd, dvd.n)], inc=(dvd, 1))
        if mamba_mode:
            E.wait("vector", gld, ld_acc)
            E.wait("vector", ld, ld.n)
            E.op("vector", lambda v, G=G: v.tensor_scalar_mul(R(5)[:, 0:G], msq[:, 0:G],
                 1.0 / 2048.0), inc=(dvd, 1))
            E.op("vector", lambda v, G=G: v.tensor_scalar_add(R(5)[:, 0:G], R(5)[:, 0:G], 1e-5),
                 waits=[(dvd, dvd.n)], inc=(dvd, 1))
            E.op("scalar", lambda s, G=G: s.activation(R(4)[:, 0:G], R(5)[:, 0:G], AF.Sqrt),
                 waits=[(dvd, dvd.n)], inc=(evd, 1))
            E.op("vector", lambda v, G=G: v.reciprocal(R(5)[:, 0:G], R(4)[:, 0:G]),
                 waits=[(evd, evd.n)], inc=(dvd, 1))
            E.op("vector", lambda v, G=G: v.tensor_copy(rowbf[:, 0:G], R(5)[:, 0:G]),
                 waits=[(dvd, dvd.n), (mmd, mmd.n)], inc=(dvd, 1))
            t0 = 0
            while t0 < G:
                tw = min(512, G - t0)
                mm_end(lambda t, t0=t0, tw=tw: t.matmul(psR[:, 0:tw], ones_r[:],
                       rowbf[:, t0:t0 + tw], start=True, stop=True),
                       waits=[(dvd, dvd.n)] if t0 == 0 else [])
                E.op("scalar", lambda s, t0=t0, tw=tw: s.activation(mrb[:, t0:t0 + tw],
                     psR[:, 0:tw], AF.Copy), waits=[(mmd, mmd.n)], inc=(evd, 1))
                t0 += tw
            ev_mrb = evd.n
            E.wait("vector", evd, ev_mrb)
            for m in range(8):
                E.op("vector", lambda v, m=m, G=G: v.tensor_mul(tchk[:, 0:G],
                     mp[:, m * G:(m + 1) * G], mrb[:, 0:G]),
                     waits=[(dvd, dvd.n)], inc=(dvd, 1))
                E.op("vector", lambda v, m=m, G=G: v.tensor_add(
                    x1g[:, m * G:(m + 1) * G], x1g[:, m * G:(m + 1) * G], tchk[:, 0:G]),
                    waits=[(dvd, dvd.n)], inc=(dvd, 1))
        E.wait("scalar", gld, gld.n)
        E.wait("scalar", ld, ld.n)
        E.barrier(bar)
        if stage < 2:
            E.op("sync", lambda s, toff=toff, G=G: s.dma_start(
                out=xo_d.rearrange("p (k t) -> p k t", k=8)[:, :, toff:toff + G],
                in_=x1g[:, 0:8 * G].rearrange("p (k t) -> p k t", k=8)), inc=(sto, 16))
            if debug and mamba_mode and gi == 0:
                E.op("sync", lambda s: s.dma_start(out=dbg["d_mp"], in_=mp[:, 0:8*512]), inc=(sto, 16))
                E.op("sync", lambda s: s.dma_start(out=dbg["d_msq"], in_=msq[:, 0:512]), inc=(sto, 16))
                E.op("sync", lambda s: s.dma_start(out=dbg["d_mrb"], in_=mrb[:, 0:512]), inc=(sto, 16))
            E.barrier(bar)
            toff += G
            continue
        for m in range(8):
            E.op("scalar", lambda s, m=m, G=G: s.activation(xbb[:, m * G:(m + 1) * G],
                 x1g[:, m * G:(m + 1) * G], AF.Copy), inc=(evd, 1))
        E.wait("tensor", evd, evd.n)
        E.wait("vector", evd, evd.n)
        for m in range(8):
            E.op("scalar", lambda s, m=m, G=G: s.activation(sq[:, 0:G],
                 xbb[:, m * G:(m + 1) * G], AF.Square),
                 waits=[(mmd, mmd.n)], inc=(evd, 1))
            E.op("tensor", lambda t, m=m, G=G: t.matmul(
                psR[0:1, 0:G], ones_c[:], xbb[:, m * G:(m + 1) * G],
                start=(m == 0), stop=(m == 7), skip_group_check=True),
                waits=[(evd, evd.n)])
            mm_end(lambda t, m=m, G=G: t.matmul(
                psQ[0:1, 0:G], ones_c[:], sq[:, 0:G],
                start=(m == 0), stop=(m == 7), skip_group_check=True))
        sm = mmd.n
        E.op("vector", lambda v, G=G: v.tensor_scalar_mul(R(2)[:, 0:G], psR[0:1, 0:G], 1.0 / 1024),
             waits=[(mmd, sm)])
        E.op("vector", lambda v, G=G: v.tensor_scalar_mul(R(3)[:, 0:G], psQ[0:1, 0:G], 1.0 / 1024))
        E.op("vector", lambda v, G=G: v.scalar_tensor_tensor(R(4)[:, 0:G], R(2)[:, 0:G], -1.0,
                                                             R(2)[:, 0:G], ALU.mult, ALU.mult))
        E.op("vector", lambda v, G=G: v.tensor_add(R(3)[:, 0:G], R(3)[:, 0:G], R(4)[:, 0:G]))
        E.op("vector", lambda v, G=G: v.tensor_scalar_add(R(3)[:, 0:G], R(3)[:, 0:G], 1e-5),
             inc=(dvd, 1))
        E.op("scalar", lambda s, G=G: s.activation(R(4)[:, 0:G], R(3)[:, 0:G], AF.Sqrt),
             waits=[(dvd, dvd.n)], inc=(evd, 1))
        E.op("vector", lambda v, G=G: v.reciprocal(R(3)[:, 0:G], R(4)[:, 0:G]),
             waits=[(evd, evd.n)])
        E.op("vector", lambda v, G=G: v.scalar_tensor_tensor(R(4)[:, 0:G], R(2)[:, 0:G], -1.0,
                                                             R(3)[:, 0:G], ALU.mult, ALU.mult))
        E.op("vector", lambda v, G=G: v.tensor_copy(rowbf[:, 0:G], R(3)[:, 0:G]),
             waits=[(mmd, mmd.n)], inc=(dvd, 1))
        t0 = 0
        while t0 < G:
            tw = min(512, G - t0)
            mm_end(lambda t, t0=t0, tw=tw: t.matmul(psR[:, 0:tw], ones_r[:],
                   rowbf[:, t0:t0 + tw], start=True, stop=True),
                   waits=[(dvd, dvd.n)] if t0 == 0 else [])
            E.op("scalar", lambda s, t0=t0, tw=tw: s.activation(bA[:, t0:t0 + tw],
                 psR[:, 0:tw], AF.Copy), waits=[(mmd, mmd.n)], inc=(evd, 1))
            t0 += tw
        E.op("vector", lambda v, G=G: v.tensor_copy(rowbf[:, 0:G], R(4)[:, 0:G]),
             waits=[(mmd, mmd.n), (evd, evd.n)], inc=(dvd, 1))
        t0 = 0
        while t0 < G:
            tw = min(512, G - t0)
            mm_end(lambda t, t0=t0, tw=tw: t.matmul(psR[:, 0:tw], ones_r[:],
                   rowbf[:, t0:t0 + tw], start=True, stop=True),
                   waits=[(dvd, dvd.n)] if t0 == 0 else [])
            E.op("scalar", lambda s, t0=t0, tw=tw: s.activation(bC[:, t0:t0 + tw],
                 psR[:, 0:tw], AF.Copy), waits=[(mmd, mmd.n)], inc=(evd, 1))
            t0 += tw
        ev_ln1 = evd.n
        if stage < 3:
            E.op("sync", lambda s, toff=toff, G=G: s.dma_start(
                out=xo_d.rearrange("p (k t) -> p k t", k=8)[:, :, toff:toff + G],
                in_=x1g[:, 0:8 * G].rearrange("p (k t) -> p k t", k=8)),
                waits=[(evd, evd.n)], inc=(sto, 16))
            E.barrier(bar)
            toff += G
            continue
        for m in range(8):
            E.op("vector", lambda v, m=m, G=G: v.tensor_mul(tchk[:, 0:G],
                 xbb[:, m * G:(m + 1) * G], bA[:, 0:G]),
                 waits=[(evd, ev_ln1)] if m == 0 else [(mmd, mmd.n)])
            E.op("vector", lambda v, m=m, G=G: v.tensor_add(tchk[:, 0:G], tchk[:, 0:G],
                 bC[:, 0:G]), inc=(dvd, 1))
            E.op("scalar", lambda s, m=m, G=G: s.activation(sq[:, 0:G], tchk[:, 0:G], AF.Square),
                 waits=[(dvd, dvd.n)], inc=(evd, 1))
            E.op("tensor", lambda t, m=m, G=G: t.matmul(
                psR[0:1, 0:G], ones_c[:], tchk[:, 0:G],
                start=(m == 0), stop=(m == 7), skip_group_check=True),
                waits=[(evd, evd.n), (dvd, dvd.n)])
            mm_end(lambda t, m=m, G=G: t.matmul(
                psQ[0:1, 0:G], ones_c[:], sq[:, 0:G],
                start=(m == 0), stop=(m == 7), skip_group_check=True))
        sm2 = mmd.n
        E.op("vector", lambda v, G=G: v.tensor_scalar_mul(R(2)[:, 0:G], psR[0:1, 0:G], 1.0 / 1024),
             waits=[(mmd, sm2)])
        E.op("vector", lambda v, G=G: v.tensor_scalar_mul(R(3)[:, 0:G], psQ[0:1, 0:G], 1.0 / 1024))
        E.op("vector", lambda v, G=G: v.scalar_tensor_tensor(R(4)[:, 0:G], R(2)[:, 0:G], -1.0,
                                                             R(2)[:, 0:G], ALU.mult, ALU.mult))
        E.op("vector", lambda v, G=G: v.tensor_add(R(3)[:, 0:G], R(3)[:, 0:G], R(4)[:, 0:G]))
        E.op("vector", lambda v, G=G: v.tensor_scalar_add(R(3)[:, 0:G], R(3)[:, 0:G], 1e-5),
             inc=(dvd, 1))
        E.op("scalar", lambda s, G=G: s.activation(R(4)[:, 0:G], R(3)[:, 0:G], AF.Sqrt),
             waits=[(dvd, dvd.n)], inc=(evd, 1))
        E.op("vector", lambda v, G=G: v.reciprocal(R(3)[:, 0:G], R(4)[:, 0:G]),
             waits=[(evd, evd.n)])
        E.op("vector", lambda v, G=G: v.scalar_tensor_tensor(R(4)[:, 0:G], R(2)[:, 0:G], -1.0,
                                                             R(3)[:, 0:G], ALU.mult, ALU.mult))
        E.op("vector", lambda v, G=G: v.tensor_copy(rowbf[:, 0:G], R(3)[:, 0:G]),
             waits=[(mmd, mmd.n)], inc=(dvd, 1))
        t0 = 0
        while t0 < G:
            tw = min(512, G - t0)
            mm_end(lambda t, t0=t0, tw=tw: t.matmul(psR[:, 0:tw], ones_r[:],
                   rowbf[:, t0:t0 + tw], start=True, stop=True),
                   waits=[(dvd, dvd.n)] if t0 == 0 else [])
            E.op("scalar", lambda s, t0=t0, tw=tw: s.activation(bA2[:, t0:t0 + tw],
                 psR[:, 0:tw], AF.Copy), waits=[(mmd, mmd.n)], inc=(evd, 1))
            t0 += tw
        E.op("vector", lambda v, G=G: v.tensor_copy(rowbf[:, 0:G], R(4)[:, 0:G]),
             waits=[(mmd, mmd.n), (evd, evd.n)], inc=(dvd, 1))
        t0 = 0
        while t0 < G:
            tw = min(512, G - t0)
            mm_end(lambda t, t0=t0, tw=tw: t.matmul(psR[:, 0:tw], ones_r[:],
                   rowbf[:, t0:t0 + tw], start=True, stop=True),
                   waits=[(dvd, dvd.n)] if t0 == 0 else [])
            E.op("scalar", lambda s, t0=t0, tw=tw: s.activation(bC2[:, t0:t0 + tw],
                 psR[:, 0:tw], AF.Copy), waits=[(mmd, mmd.n)], inc=(evd, 1))
            t0 += tw
        ev_ln2 = evd.n
        for m in range(8):
            E.op("vector", lambda v, m=m, G=G: v.tensor_mul(tchk[:, 0:G],
                 xbb[:, m * G:(m + 1) * G], bA[:, 0:G]),
                 waits=[(evd, ev_ln2)] if m == 0 else [(evd, evd.n)])
            E.op("vector", lambda v, m=m, G=G: v.tensor_add(tchk[:, 0:G], tchk[:, 0:G],
                 bC[:, 0:G]))
            E.op("vector", lambda v, m=m, G=G: v.tensor_mul(tchk[:, 0:G], tchk[:, 0:G],
                 bA2[:, 0:G]))
            E.op("vector", lambda v, m=m, G=G: v.tensor_add(tchk[:, 0:G], tchk[:, 0:G],
                 bC2[:, 0:G]), inc=(dvd, 1))
            E.op("scalar", lambda s, m=m, G=G: s.activation(
                nxl[:, m * G:(m + 1) * G], tchk[:, 0:G], AF.Identity,
                scale=lng[:, m:m + 1], bias=lnb[:, m:m + 1]),
                waits=[(dvd, dvd.n)], inc=(evd, 1))
        ev_nxl = evd.n
        E.barrier(bar)
        if stage < 4:
            E.op("sync", lambda s, toff=toff, G=G: s.dma_start(
                out=xo_d.rearrange("p (k t) -> p k t", k=8)[:, :, toff:toff + G],
                in_=x1g[:, 0:8 * G].rearrange("p (k t) -> p k t", k=8)), inc=(sto, 16))
            E.barrier(bar)
            toff += G
            continue
        E.wait("tensor", evd, ev_nxl)
        for tc in range(ntc):
            E.wait("tensor", dvd, dvd.n)
            for k in range(8):
                E.op("tensor", lambda t, tc=tc, k=k, G=G: t.matmul(
                    psR[:, 0:8],
                    nxl[:, k * G + tc * 128: k * G + (tc + 1) * 128],
                    rw[:, k * 8:(k + 1) * 8],
                    start=(k == 0), stop=(k == 7), skip_group_check=True),
                    **({"inc": (mmd, 1)} if k == 7 else {}))
            def vch(fn):
                E.op("vector", fn, waits=[(dvd, dvd.n)], inc=(dvd, 1))
            E.op("vector", lambda v, tc=tc: v.tensor_copy(prb[:, 0:8], psR[:, 0:8]),
                 waits=[(mmd, mmd.n)], inc=(dvd, 1))
            vch(lambda v: v.reduce_max(pcol[:, 0:1], prb[:, 0:8], axis=mybir.AxisListType.X))
            vch(lambda v: v.tensor_scalar_mul(pcol[:, 1:2], pcol[:, 0:1], -1.0))
            E.op("scalar", lambda s: s.activation(prb[:, 8:16], prb[:, 0:8], AF.Exp,
                 bias=pcol[:, 1:2]), waits=[(dvd, dvd.n)], inc=(evd, 1))
            E.op("vector", lambda v: v.reduce_max(pcol[:, 2:3], prb[:, 8:16],
                 axis=mybir.AxisListType.X), waits=[(evd, evd.n)], inc=(dvd, 1))
            vch(lambda v: v.tensor_scalar(prb[:, 16:24], prb[:, 8:16],
                 pcol[:, 2:3], None, ALU.is_ge))
            vch(lambda v: v.scalar_tensor_tensor(prb[:, 24:32], prb[:, 16:24],
                 -1e9, prb[:, 8:16], ALU.mult, ALU.add))
            vch(lambda v: v.reduce_max(pcol[:, 3:4], prb[:, 24:32], axis=mybir.AxisListType.X))
            vch(lambda v: v.tensor_scalar(prb[:, 16:24], prb[:, 8:16],
                 pcol[:, 3:4], None, ALU.is_ge))
            vch(lambda v: v.tensor_mul(prb[:, 24:32], prb[:, 8:16], prb[:, 16:24]))
            vch(lambda v: v.reduce_sum(pcol[:, 4:5], prb[:, 24:32], axis=mybir.AxisListType.X))
            vch(lambda v: v.reciprocal(pcol[:, 5:6], pcol[:, 4:5]))
            vch(lambda v: v.tensor_scalar_mul(prb[:, 24:32], prb[:, 24:32], pcol[:, 5:6]))
            mm_end(lambda t: t.transpose(psQ[0:8, 0:128], prb[:, 24:32], eyef[:]),
                   waits=[(dvd, dvd.n)])
            E.op("vector", lambda v, tc=tc: v.tensor_copy(wfr[0:8, tc * 128:(tc + 1) * 128],
                 psQ[0:8, 0:128]), waits=[(mmd, mmd.n)], inc=(dvd, 1))
        E.barrier(bar)
        if debug and gi == 0:
            E.op("sync", lambda s: s.dma_start(out=dbg["d_prb"], in_=prb[:]), inc=(sto, 16))
            E.op("sync", lambda s: s.dma_start(out=dbg["d_pcol"], in_=pcol[:]), inc=(sto, 16))
            E.op("sync", lambda s: s.dma_start(out=dbg["d_x1"], in_=x1g[:, 0:8*512]), inc=(sto, 16))
            E.op("sync", lambda s: s.dma_start(out=dbg["d_nxl"], in_=nxl[:, 0:8*512]), inc=(sto, 16))
            E.op("gpsimd", lambda g: g.dma_start(out=dbg["d_wfr"], in_=wfr[:, 0:512]), inc=(sto, 16))
            E.barrier(bar)

        for e in (range(nexp) if stage >= 5 else range(0)):
            E.wait("tensor", dvd, dvd.n)
            t0 = 0
            while t0 < G:
                tw = min(512, G - t0)
                mm_end(lambda t, t0=t0, tw=tw, e=e: t.matmul(psQ[:, 0:tw],
                       oneh[:, e * 128:(e + 1) * 128], wfr[:, t0:t0 + tw],
                       start=True, stop=True))
                E.op("scalar", lambda s, t0=t0, tw=tw: s.activation(wbc[:, t0:t0 + tw],
                     psQ[:, 0:tw], AF.Copy), waits=[(mmd, mmd.n)], inc=(evd, 1))
                t0 += tw
            matmul_phase(up_d[e], 32, 8, nxl, G,
                         act_writer(Ab, upb, e * 32, AF.Gelu, G), G,
                         pre_waits=[(evd, evd.n)])
            if e % 4 == 0:
                i = e // 4
                matmul_phase(s0_d[i], 32, 32, Ab, G,
                             act_writer(Bb, s0b, i * 32, AF.Identity, G), G)
                ln_inline(Bb, Ab, G, l0g, i * 32, l0b, i * 32, 32, gelu=True)
                h2 = Ab
            elif e % 4 == 1:
                i = e // 4
                matmul_phase(s1a_d[i], 16, 32, Ab, G,
                             act_writer(Bb, s1ab, i * 16, AF.Gelu, G), G)
                matmul_phase(s1b_d[i], 32, 16, Bb, G,
                             act_writer(Ab, s1bb, i * 32, AF.Identity, G), G)
                ln_inline(Ab, Bb, G, l1g, i * 32, l1b, i * 32, 32, gelu=False)
                h2 = Bb
            elif e % 4 == 2:
                i = e // 4
                matmul_phase(s2_d[i], 32, 32, Ab, G,
                             act_writer(Bb, s2b, i * 32, AF.Gelu, G), G)
                h2 = Bb
            else:
                h2 = Ab
            def down_writer(m, tiles, mmc, e=e):
                for ti, (ps, t0, tw) in enumerate(tiles):
                    E.op("vector", lambda v, m=m, ps=ps, t0=t0, tw=tw, e=e: v.scalar_tensor_tensor(
                        dtmp[:, 0:tw], ps[:, 0:tw], dnb[:, e * 8 + m: e * 8 + m + 1],
                        wbc[:, t0:t0 + tw], ALU.add, ALU.mult),
                        waits=[(mmd, mmc)] if ti == 0 else [],
                        inc=(wfree, 1) if ti == 0 else None)
                    def addf(v, m=m, t0=t0, tw=tw, G=G):
                        return v.tensor_add(x1g[:, m * G + t0: m * G + t0 + tw],
                                            x1g[:, m * G + t0: m * G + t0 + tw],
                                            dtmp[:, 0:tw])
                    E.op("vector", addf, inc=(evd, 1) if ti == len(tiles) - 1 else None)
            matmul_phase(dn_d[e], 8, 32, h2, G, down_writer, G)
            E.barrier(bar)
            if debug and gi == 0 and e == 0:
                E.op("sync", lambda s: s.dma_start(out=dbg["d_wbc"], in_=wbc[:, 0:512]), inc=(sto, 16))
                E.op("sync", lambda s: s.dma_start(out=dbg["d_A"], in_=Ab[:, 0:32*512]), inc=(sto, 16))
                E.op("sync", lambda s: s.dma_start(out=dbg["d_B"], in_=Bb[:, 0:32*512]), inc=(sto, 16))
                E.op("sync", lambda s: s.dma_start(out=dbg["d_xacc"], in_=x1g[:, 0:8*512]), inc=(sto, 16))
                E.barrier(bar)
        for m in range(8):
            E.op("vector", lambda v, m=m, G=G: v.tensor_mul(tchk[:, 0:G],
                 xbb[:, m * G:(m + 1) * G], bA[:, 0:G]),
                 waits=[(dvd, dvd.n)], inc=(dvd, 1))
            E.op("vector", lambda v, m=m, G=G: v.tensor_add(tchk[:, 0:G], tchk[:, 0:G],
                 bC[:, 0:G]), waits=[(dvd, dvd.n)], inc=(dvd, 1))
            E.op("vector", lambda v, m=m, G=G: v.tensor_add(
                x1g[:, m * G:(m + 1) * G], x1g[:, m * G:(m + 1) * G], tchk[:, 0:G]),
                waits=[(dvd, dvd.n)], inc=(dvd, 1))
        E.op("sync", lambda s, toff=toff, G=G: s.dma_start(
            out=xo_d.rearrange("p (k t) -> p k t", k=8)[:, :, toff:toff + G],
            in_=x1g[:, 0:8 * G].rearrange("p (k t) -> p k t", k=8)),
            waits=[(dvd, dvd.n)], inc=(sto, 16))
        E.barrier(bar)
        toff += G

    E.op("sync", lambda s: s.wait_ge(sto.h, sto.n))
    E.replay(block)
    es.close()
    return nc


def stage_ffn_weights(z):
    g = lambda k: np.asarray(z[k], np.float32)
    upW, dnW = g("L0_ffn_up_W"), g("L0_ffn_down_W")
    s0W, s1aW = g("L0_ffn_spec0_W"), g("L0_ffn_spec1a_W")
    s1bW, s2W = g("L0_ffn_spec1b_W"), g("L0_ffn_spec2_W")
    d = {}
    for e in range(8):
        d[f"upW{e}"] = col_layout(upW[e])
        d[f"dnW{e}"] = col_layout(dnW[e])
    for i in range(2):
        d[f"s0W{i}"] = col_layout(s0W[i])
        d[f"s1aW{i}"] = col_layout(s1aW[i])
        d[f"s1bW{i}"] = col_layout(s1bW[i])
        d[f"s2W{i}"] = col_layout(s2W[i])
    d["upb"] = np.concatenate([vec_cols(g("L0_ffn_up_b")[e]) for e in range(8)], axis=1)
    d["dnb"] = np.concatenate([vec_cols(g("L0_ffn_down_b")[e]) for e in range(8)], axis=1)
    d["s0b"] = np.concatenate([vec_cols(g("L0_ffn_spec0_b")[i]) for i in range(2)], axis=1)
    d["s1ab"] = np.concatenate([vec_cols(g("L0_ffn_spec1a_b")[i]) for i in range(2)], axis=1)
    d["s1bb"] = np.concatenate([vec_cols(g("L0_ffn_spec1b_b")[i]) for i in range(2)], axis=1)
    d["s2b"] = np.concatenate([vec_cols(g("L0_ffn_spec2_b")[i]) for i in range(2)], axis=1)
    d["l0g"] = np.concatenate([vec_cols(g("L0_ffn_ln0_g")[i]) for i in range(2)], axis=1)
    d["l0b"] = np.concatenate([vec_cols(g("L0_ffn_ln0_b")[i]) for i in range(2)], axis=1)
    d["l1g"] = np.concatenate([vec_cols(g("L0_ffn_ln1_g")[i]) for i in range(2)], axis=1)
    d["l1b"] = np.concatenate([vec_cols(g("L0_ffn_ln1_b")[i]) for i in range(2)], axis=1)
    d["lng"] = vec_cols(g("L0_ffn_ln_g"))
    d["lnb"] = vec_cols(g("L0_ffn_ln_b"))
    rwv = g("L0_ffn_router_w")
    d["rw"] = np.ascontiguousarray(rwv.reshape(8, 128, 8).transpose(1, 0, 2).reshape(128, 64))
    d["eyef"] = EYE128
    oh = np.zeros((8, 8, 128), np.float32)
    for e in range(8):
        oh[e, e, :] = 1.0
    d["oneh"] = np.ascontiguousarray(oh.transpose(1, 0, 2).reshape(8, 8 * 128))
    return d


def build_L3(ngroups=16, debug=False, stage=99):
    T = ngroups * 512
    NCH = 4
    nc = bass.Bass("TRN2", target_bir_lowering=False, detect_race_conditions=False)

    def dram(name, shape, dt=F32, kind="ExternalInput"):
        return nc.dram_tensor(name, shape, dt, kind=kind).ap()

    x_d = dram("x", [128, 8 * 8192])
    wz_d = dram("wz", [4, 128, 1024])
    wx_d = dram("wx", [4, 128, 1024])
    wbc_d = dram("wbc", [2, 128, 1024])
    wdt_d = dram("wdt", [1, 128, 64])
    cx_d = dram("cx", [128, 2048])
    cb_d = dram("cb", [128, 1024])
    cxb_d = dram("cxb", [128, 4])
    cbb_d = dram("cbb", [128, 2])
    dtb_d = dram("dtb", [8, 1])
    alog_d = dram("alog", [8, 1])
    dtok_d = dram("dtok", [128, 8])
    nwc_d = dram("nwc", [128, 4])
    lnw_d = dram("lnw", [128, 8])
    lnb_d = dram("lnb", [128, 8])
    wo_d = dram("wo", [8, 128, 512])
    mlow_d = dram("mlow", [128, 128])
    eye_d = dram("eye", [128, 128])
    oneh8_d = dram("oneh8", [8, 8 * 128])
    eyehi_d = dram("eyehi", [128, 64])
    y_d = dram("ypart", [128, 8 * T], F32, kind="ExternalOutput")
    ysq_d = dram("ysq", [1, T], F32, kind="ExternalOutput")
    dbg = {}
    if debug:
        for nm, p_, w, dt in [("d_ug", 128, 8*512, BF16), ("d_zg", 128, 4*512, BF16),
                              ("d_xsg", 128, 4*512, BF16), ("d_bcg", 128, 2*512, BF16),
                              ("d_dt8", 8, 512, F32), ("d_s8", 8, 512, F32),
                              ("d_trc", 128, 24, F32), ("d_Gt", 128, 128, BF16),
                              ("d_expd", 128, 128, BF16), ("d_xst", 128, 512, BF16),
                              ("d_yt0", 128, 512, F32), ("d_yfm", 128, 4*512, F32),
                              ("d_H", 64, 512, BF16), ("d_ygat", 128, 4*512, F32)]:
            dbg[nm] = nc.dram_tensor(nm, [p_, w], dt, kind="ExternalOutput").ap()

    es = ExitStack()
    sbt = lambda n_, s_, dt=F32: es.enter_context(nc.sbuf_tensor(n_, s_, dt))
    pst = lambda n_, s_, dt=F32: es.enter_context(nc.psum_tensor(n_, s_, dt))

    wz = sbt("wz_s", [128, 4096], BF16)
    wx = sbt("wx_s", [128, 4096], BF16)
    wbc = sbt("wbc_s", [128, 2048], BF16)
    wdt = sbt("wdt_s", [128, 64], BF16)
    cx = sbt("cx_s", [128, 2048], BF16)
    cb = sbt("cb_s", [128, 1024], BF16)
    cxb = sbt("cxb_s", [128, 4])
    cbb = sbt("cbb_s", [128, 2])
    dtb = sbt("dtb_s", [8, 1])
    alog = sbt("alog_s", [8, 1])
    nega = sbt("nega_s", [8, 1])
    dtok = sbt("dtok_s", [128, 8])
    nwc = sbt("nwc_s", [128, 4])
    lnw = sbt("lnw_s", [128, 8])
    lnb = sbt("lnb_s", [128, 8])
    wo = sbt("wo_s", [128, 4096], BF16)
    mlow = sbt("mlow_s", [128, 128])
    eyeb = sbt("eyeb_s", [128, 128], BF16)
    eyef = sbt("eyef_s", [128, 128])
    ones_c = sbt("ones_c", [128, 1], BF16)
    ones_r = sbt("ones_r", [1, 128], BF16)
    ones8 = sbt("ones8", [8, 512])
    oneh8 = sbt("oneh8_s", [8, 8 * 128], BF16)
    oneh8f = sbt("oneh8f_s", [8, 8 * 128])

    xg = sbt("xg", [128, 8 * 512])
    xb = sbt("xb", [128, 8 * 512], BF16)
    ug = sbt("ug", [128, 8 * 512], BF16)
    zg = sbt("zg", [128, 4 * 512], BF16)
    xsp = sbt("xsp", [128, 4 * 515], BF16)
    bcp = sbt("bcp", [128, 2 * 515], BF16)
    xsg = sbt("xsg", [128, 4 * 512], BF16)
    bcg = sbt("bcg", [128, 2 * 512], BF16)
    sq = sbt("sqb", [128, 512], BF16)
    rows = sbt("rows", [1, 6 * 512])
    rowbf = sbt("rowbf", [1, 512], BF16)
    bA = sbt("bA", [128, 512], BF16)
    bC = sbt("bC", [128, 512], BF16)
    dtr8 = sbt("dtr8", [8, 512])
    dt8 = sbt("dt8", [8, 512])
    a8 = sbt("a8", [8, 512])
    s8 = sbt("s8", [8, 512])
    lms8 = sbt("lms8", [8, 512])
    t8a = sbt("t8a", [8, 512])
    t8b = sbt("t8b", [8, 512])
    s8b = sbt("s8b", [8, 512], BF16)
    lms8b = sbt("lms8b", [8, 512], BF16)
    scar = sbt("scar", [8, 1])
    sprev = sbt("sprev", [8, 1])
    er8 = sbt("er8", [8, 128])
    zf8 = sbt("zf8", [8, 128])
    hf8 = sbt("hf8", [8, 128])
    trc = sbt("trc", [128, 24])
    bt = sbt("btk", [128, 64], BF16)
    xst = [sbt(f"xst{i}", [128, 8 * 64], BF16) for i in range(NCH)]
    Gt = sbt("Gt", [128, 128], BF16)
    expd = sbt("expd", [128, 128], BF16)
    H_sb = sbt("H_sb", [64, 512], BF16)
    Ihf = sbt("Ihf", [64, 64], BF16)
    Zh = sbt("Zh", [128, 64], BF16)
    eyeb64 = sbt("eyeb64", [64, 64], BF16)
    eyehi = sbt("eyehi_s", [128, 64], BF16)
    ytok = [sbt(f"ytok{i}", [128, 8 * 64], BF16) for i in range(NCH)]
    yfm = sbt("yfm", [128, 4 * 512], BF16)
    ygat = sbt("ygat", [128, 4 * 512])
    ynb = sbt("ynb", [128, 4 * 512], BF16)
    ysqrow = sbt("ysqrow", [1, 512])
    yg = sbt("yg", [128, 8 * 512])

    psD_ = pst("psD_", [128, 128])
    psB_ = pst("psB_", [128, 128])
    psY_ = pst("psY_", [128, 64])
    psC_ = pst("psC_", [128, 64])
    psH_ = pst("psH_", [128, 512])
    psT_ = pst("psT_", [128, 512], BF16)
    psTf = pst("psTf", [128, 512])
    psP_ = pst("psP_", [128, 512])

    sems = {}
    for s in ("ld", "gld", "mmd", "evd", "dvd", "sto", "bar"):
        sems[s] = Sem(es.enter_context(nc.semaphore(s)), s)
    ld, gld, mmd, evd, dvd, sto, bar = (
        sems[k] for k in ("ld", "gld", "mmd", "evd", "dvd", "sto", "bar"))
    block = es.enter_context(nc.Block())
    E = Emit(nc)
    R = lambda i: rows[:, i * 512:(i + 1) * 512]
    E.enable_vchain(dvd)


    def mm_end(fn, waits=()):
        return E.op("tensor", fn, waits=waits, inc=(mmd, 1))

    for t_, src in [(cxb, cxb_d), (cbb, cbb_d), (dtb, dtb_d), (alog, alog_d),
                    (nwc, nwc_d), (lnw, lnw_d), (lnb, lnb_d), (mlow, mlow_d),
                    (eyef, eye_d), (ones8, None)]:
        if src is None:
            continue
        E.op("sync", lambda s, t_=t_, src=src: s.dma_start(out=t_[:], in_=src), inc=(ld, 16))
    for t_, src, m in [(wz, wz_d, 4), (wx, wx_d, 4), (wbc, wbc_d, 2), (wdt, wdt_d, 1),
                       (wo, wo_d, 8)]:
        E.op("gpsimd", lambda g, t_=t_, src=src, m=m: g.dma_start(
            out=t_[:].rearrange("p (m k) -> p m k", m=m),
            in_=src.rearrange("m p k -> p m k")), inc=(gld, 16))
    for t_, src in [(cx, cx_d), (cb, cb_d), (eyeb, eye_d), (oneh8, oneh8_d),
                    (dtok, dtok_d), (eyeb64, None)]:
        if src is None:
            continue
        E.op("gpsimd", lambda g, t_=t_, src=src: g.dma_start(out=t_[:], in_=src), inc=(gld, 16))
    E.op("gpsimd", lambda g: g.dma_start(out=eyeb64[:], in_=eye_d[0:64, 0:64]), inc=(gld, 16))
    E.op("gpsimd", lambda g: g.dma_start(out=eyehi[:], in_=eyehi_d), inc=(gld, 16))
    E.op("sync", lambda s: s.dma_start(out=oneh8f[:], in_=oneh8_d), inc=(ld, 16))
    for t_ in (ones_c, ones_r):
        E.op("vector", lambda v, t_=t_: v.memset(t_[:], 1.0), inc=(dvd, 1))
    E.op("vector", lambda v: v.memset(ones8[:], 1.0), inc=(dvd, 1))
    for t_ in (xsp, bcp, scar, H_sb, sprev):
        E.op("vector", lambda v, t_=t_: v.memset(t_[:], 0.0), inc=(dvd, 1))
    E.wait("scalar", ld, ld.n)
    E.op("scalar", lambda s: s.activation(nega[:], alog[:], AF.Exp), inc=(evd, 1))
    E.op("vector", lambda v: v.tensor_scalar_mul(nega[:], nega[:], -1.0),
         waits=[(evd, evd.n)], inc=(dvd, 1))
    for e in ("tensor", "scalar", "vector"):
        E.wait(e, ld, ld.n)
        E.wait(e, gld, gld.n)
        E.wait(e, dvd, dvd.n)
    E.barrier(bar)

    for g in range(ngroups):
        E.op("sync", lambda s, g=g: s.dma_start(
            out=xg[:].rearrange("p (k t) -> p k t", k=8),
            in_=x_d.rearrange("p (k t) -> p k t", k=8)[:, :, g * 512:(g + 1) * 512]),
            inc=(ld, 16))
        E.op("gpsimd", lambda gp, g=g: gp.dma_start(
            out=xb[:].rearrange("p (k t) -> p k t", k=8),
            in_=x_d.rearrange("p (k t) -> p k t", k=8)[:, :, g * 512:(g + 1) * 512]),
            inc=(gld, 16))
        E.wait("tensor", ld, ld.n)
        E.wait("tensor", gld, gld.n)
        E.wait("scalar", gld, gld.n)
        for k in range(8):
            E.op("tensor", lambda t, k=k: t.matmul(
                psTf[0:1, :], ones_c[:], xb[:, k * 512:(k + 1) * 512],
                start=(k == 0), stop=(k == 7), skip_group_check=True))
        for k in range(8):
            E.op("scalar", lambda s, k=k: s.activation(
                sq[:], xb[:, k * 512:(k + 1) * 512], AF.Square),
                waits=[(mmd, mmd.n)] if k > 0 else [], inc=(evd, 1))
            mm_end(lambda t, k=k: t.matmul(
                psP_[0:1, :], ones_c[:], sq[:],
                start=(k == 0), stop=(k == 7), skip_group_check=True),
                waits=[(evd, evd.n)])
        sm = mmd.n
        E.op("vector", lambda v: v.tensor_scalar_mul(R(2), psTf[0:1, :], 1.0 / 1024),
             waits=[(mmd, sm)])
        E.op("vector", lambda v: v.tensor_scalar_mul(R(3), psP_[0:1, :], 1.0 / 1024))
        E.op("vector", lambda v: v.scalar_tensor_tensor(R(4), R(2), -1.0, R(2), ALU.mult, ALU.mult))
        E.op("vector", lambda v: v.tensor_add(R(3), R(3), R(4)))
        E.op("vector", lambda v: v.tensor_scalar_add(R(3), R(3), 1e-5), inc=(dvd, 1))
        E.op("scalar", lambda s: s.activation(R(4), R(3), AF.Sqrt),
             waits=[(dvd, dvd.n)], inc=(evd, 1))
        E.op("vector", lambda v: v.reciprocal(R(3), R(4)), waits=[(evd, evd.n)])
        E.op("vector", lambda v: v.scalar_tensor_tensor(R(4), R(2), -1.0, R(3), ALU.mult, ALU.mult))
        E.op("vector", lambda v: v.tensor_copy(rowbf[:], R(3)),
             waits=[(mmd, mmd.n)], inc=(dvd, 1))
        mm_end(lambda t: t.matmul(psTf[:, :], ones_r[:], rowbf[:], start=True, stop=True),
               waits=[(dvd, dvd.n)])
        E.op("scalar", lambda s: s.activation(bA[:], psTf[:, :], AF.Copy),
             waits=[(mmd, mmd.n)], inc=(evd, 1))
        E.op("vector", lambda v: v.tensor_copy(rowbf[:], R(4)),
             waits=[(mmd, mmd.n), (evd, evd.n)], inc=(dvd, 1))
        mm_end(lambda t: t.matmul(psTf[:, :], ones_r[:], rowbf[:], start=True, stop=True),
               waits=[(dvd, dvd.n)])
        E.op("scalar", lambda s: s.activation(bC[:], psTf[:, :], AF.Copy),
             waits=[(mmd, mmd.n)], inc=(evd, 1))
        ev_bc = evd.n
        for k in range(8):
            E.op("vector", lambda v, k=k: v.tensor_mul(
                sq[:], xb[:, k * 512:(k + 1) * 512], bA[:]),
                waits=[(evd, ev_bc)] if k == 0 else [(evd, evd.n)])
            E.op("vector", lambda v, k=k: v.tensor_add(sq[:], sq[:], bC[:]), inc=(dvd, 1))
            E.op("scalar", lambda s, k=k: s.activation(
                ug[:, k * 512:(k + 1) * 512], sq[:], AF.Identity,
                scale=lnw[:, k:k + 1], bias=lnb[:, k:k + 1]),
                waits=[(dvd, dvd.n)], inc=(evd, 1))
        E.wait("tensor", evd, evd.n)
        def proj(wt, mc, dst, dst_off, dst_stride, psum, M=128):
            for m in range(mc):
                for k in range(8):
                    E.op("tensor", lambda t, wt=wt, m=m, k=k, M=M: t.matmul(
                        psum[0:M, 0:512], wt[:, m * 1024 + k * 128: m * 1024 + k * 128 + M]
                        if M != 128 else wt[:, m * 1024 + k * 128: m * 1024 + (k + 1) * 128],
                        ug[:, k * 512:(k + 1) * 512], start=(k == 0), stop=(k == 7)),
                        **({"inc": (mmd, 1)} if k == 7 else {}))
                E.op("scalar", lambda s, m=m: s.activation(
                    dst[:, dst_off + m * dst_stride: dst_off + m * dst_stride + 512],
                    psum[0:M, 0:512] if M != 128 else psum[:, 0:512], AF.Copy),
                    waits=[(mmd, mmd.n)], inc=(evd, 1))
                E.wait("tensor", evd, evd.n)
        proj(wz, 4, zg, 0, 512, psP_)
        E.op("vector", lambda v: v.tensor_copy(
            xsp[:].rearrange("p (m w) -> p m w", m=4)[:, :, 0:3],
            xsp[:].rearrange("p (m w) -> p m w", m=4)[:, :, 512:515]), inc=(dvd, 1))
        E.op("vector", lambda v: v.tensor_copy(
            bcp[:].rearrange("p (m w) -> p m w", m=2)[:, :, 0:3],
            bcp[:].rearrange("p (m w) -> p m w", m=2)[:, :, 512:515]), inc=(dvd, 1))
        E.wait("scalar", dvd, dvd.n)
        proj(wx, 4, xsp, 3, 515, psP_)
        proj(wbc, 2, bcp, 3, 515, psP_)
        for k in range(8):
            E.op("tensor", lambda t, k=k: t.matmul(
                psP_[0:8, 0:512], wdt[:, k * 8:(k + 1) * 8],
                ug[:, k * 512:(k + 1) * 512], start=(k == 0), stop=(k == 7),
                skip_group_check=True), **({"inc": (mmd, 1)} if k == 7 else {}))
        E.op("vector", lambda v: v.tensor_copy(dtr8[:], psP_[0:8, 0:512]),
             waits=[(mmd, mmd.n)], inc=(dvd, 1))
        E.op("scalar", lambda s: s.activation(t8a[:], dtr8[:], AF.Abs, bias=dtb[:, 0:1]),
             waits=[(dvd, dvd.n)], inc=(evd, 1))
        E.op("scalar", lambda s: s.activation(t8a[:], t8a[:], AF.Exp, scale=-1.0))
        E.op("scalar", lambda s: s.activation(t8a[:], t8a[:], AF.Ln, bias=1.0))
        E.op("scalar", lambda s: s.activation(t8b[:], dtr8[:], AF.Relu, bias=dtb[:, 0:1]),
             inc=(evd, 1))
        E.op("vector", lambda v: v.tensor_add(dt8[:], t8a[:], t8b[:]),
             waits=[(evd, evd.n)], inc=(dvd, 1))
        E.op("scalar", lambda s: s.activation(t8b[:], dt8[:], AF.Ln),
             waits=[(dvd, dvd.n)], inc=(evd, 1))
        E.op("vector", lambda v: v.tensor_scalar_mul(a8[:], dt8[:], nega[:, 0:1]),
             waits=[(evd, evd.n)], inc=(dvd, 1))
        E.op("vector", lambda v: v.tensor_tensor_scan(
            s8[:], ones8[:], a8[:], scar[:, 0:1], ALU.mult, ALU.add), waits=[(dvd, dvd.n)])
        E.op("vector", lambda v: v.tensor_sub(lms8[:], t8b[:], s8[:]))
        E.op("vector", lambda v: v.tensor_copy(scar[:], s8[:, 511:512]))
        E.op("vector", lambda v: v.tensor_copy(s8b[:], s8[:]))
        E.op("vector", lambda v: v.tensor_copy(lms8b[:], lms8[:]), inc=(dvd, 1))
        rows_dv = dvd.n
        E.wait("tensor", dvd, rows_dv)
        for m in range(4):
            for sft in range(4):
                E.op("tensor", lambda t, m=m, sft=sft: t.matmul(
                    psP_[:, 0:512], cx[:, (sft * 4 + m) * 128:(sft * 4 + m + 1) * 128],
                    xsp[:, m * 515 + sft: m * 515 + sft + 512],
                    start=(sft == 0), stop=(sft == 3)),
                    **({"inc": (mmd, 1)} if sft == 3 else {}))
            E.op("scalar", lambda s, m=m: s.activation(
                xsg[:, m * 512:(m + 1) * 512], psP_[:, 0:512], AF.Silu,
                bias=cxb[:, m:m + 1]), waits=[(mmd, mmd.n)], inc=(evd, 1))
            E.wait("tensor", evd, evd.n)
        for m in range(2):
            for sft in range(4):
                E.op("tensor", lambda t, m=m, sft=sft: t.matmul(
                    psP_[:, 0:512], cb[:, (sft * 2 + m) * 128:(sft * 2 + m + 1) * 128],
                    bcp[:, m * 515 + sft: m * 515 + sft + 512],
                    start=(sft == 0), stop=(sft == 3)),
                    **({"inc": (mmd, 1)} if sft == 3 else {}))
            E.op("scalar", lambda s, m=m: s.activation(
                bcg[:, m * 512:(m + 1) * 512], psP_[:, 0:512], AF.Silu,
                bias=cbb[:, m:m + 1]), waits=[(mmd, mmd.n)], inc=(evd, 1))
            E.wait("tensor", evd, evd.n)
        E.barrier(bar)

        for c in (range(NCH) if stage >= 2 else range(0)):
            CC = lambda buf, off, c=c: buf[:, off + c * 128: off + (c + 1) * 128]
            E.op("vector", lambda v, c=c: v.tensor_scalar(
                er8[:], s8[:, c * 128:(c + 1) * 128], sprev[:, 0:1], None, ALU.subtract),
                waits=[(dvd, dvd.n), (mmd, mmd.n)], inc=(dvd, 1))
            E.op("scalar", lambda s: s.activation(er8[:], er8[:], AF.Exp),
                 waits=[(dvd, dvd.n)], inc=(evd, 1))
            E.op("vector", lambda v, c=c: v.tensor_copy(t8a[:, 0:1], s8[:, c * 128 + 127: c * 128 + 128]),
                 inc=(dvd, 1))
            E.op("vector", lambda v, c=c: v.tensor_scalar(
                zf8[:], lms8[:, c * 128:(c + 1) * 128], t8a[:, 0:1], None, ALU.add),
                waits=[(dvd, dvd.n)], inc=(dvd, 1))
            E.op("scalar", lambda s: s.activation(zf8[:], zf8[:], AF.Exp),
                 waits=[(dvd, dvd.n)], inc=(evd, 1))
            E.op("vector", lambda v: v.tensor_scalar(
                t8b[:, 0:128], ones8[:, 0:128], t8a[:, 0:1], None, ALU.mult),
                waits=[(evd, evd.n)], inc=(dvd, 1))
            E.op("vector", lambda v: v.tensor_scalar(
                t8b[:, 0:128], t8b[:, 0:128], sprev[:, 0:1], None, ALU.subtract),
                inc=(dvd, 1))
            E.op("scalar", lambda s: s.activation(hf8[:], t8b[:, 0:128], AF.Exp),
                 waits=[(dvd, dvd.n)], inc=(evd, 1))
            E.op("vector", lambda v: v.tensor_copy(sprev[:], t8a[:, 0:1]),
                 waits=[(evd, evd.n)], inc=(dvd, 1))
            ev_rows = evd.n
            E.op("vector", lambda v: v.tensor_copy(t8a[:, 0:128].bitcast(F32) if False else t8a[:, 0:128], er8[:]), inc=(dvd, 1))
            E.wait("tensor", evd, ev_rows)
            E.wait("tensor", dvd, dvd.n)
            mm_end(lambda t: t.transpose(psTf[0:128, 0:8], er8[:], eyef[0:8, 0:8]))
            mm_end(lambda t: t.transpose(psTf[0:128, 8:16], zf8[:], eyef[0:8, 0:8]))
            mm_end(lambda t: t.transpose(psTf[0:128, 16:24], hf8[:], eyef[0:8, 0:8]))
            E.op("vector", lambda v: v.tensor_copy(trc[:], psTf[:, 0:24]),
                 waits=[(mmd, mmd.n)], inc=(dvd, 1))
            trc_dv = dvd.n
            E.op("tensor", lambda t, c=c: t.matmul(
                psB_[:], bcg[0:64, c * 128:(c + 1) * 128],
                bcg[0:64, 512 + c * 128: 512 + (c + 1) * 128], start=True, stop=True),
                waits=[(dvd, trc_dv)])
            bct_mm = mm_end(lambda t: t.nop())
            E.op("vector", lambda v: v.tensor_copy(Gt[:], psB_[:]),
                 waits=[(mmd, bct_mm)], inc=(dvd, 1))
            bct_dv = dvd.n
            E.op("tensor", lambda t, c=c: t.transpose(
                psT_[:, 0:64], bcg[0:64, c * 128:(c + 1) * 128], eyeb[0:64, 0:64]),
                waits=[(dvd, bct_dv)])
            bt_mm = mm_end(lambda t: t.nop())
            E.op("scalar", lambda s: s.activation(bt[:], psT_[:, 0:64], AF.Copy),
                 waits=[(mmd, bt_mm)], inc=(evd, 1))
            E.wait("tensor", evd, evd.n)
            for h in range(8):
                E.op("tensor", lambda t, c=c, h=h: t.transpose(
                    psT_[:, 64:128],
                    xsg[(h % 2) * 64:(h % 2) * 64 + 64,
                        (h // 2) * 512 + c * 128:(h // 2) * 512 + (c + 1) * 128],
                    eyeb[0:64, 0:64] if h % 2 == 0 else eyehi[64:128, 0:64]))
                xt_mm = mm_end(lambda t: t.nop())
                E.op("scalar", lambda s, c=c, h=h: s.activation(
                    xst[c][:, h * 64:(h + 1) * 64], psT_[:, 64:128], AF.Copy),
                    waits=[(mmd, xt_mm)], inc=(evd, 1))
                E.wait("tensor", evd, evd.n)
            ev_xt = evd.n
            for h in range(8):
                first = (h == 0)
                E.op("tensor", lambda t, c=c, h=h: t.matmul(
                    psD_[:], oneh8f[:, h * 128:(h + 1) * 128],
                    s8[:, c * 128:(c + 1) * 128], start=True, stop=False,
                    skip_group_check=True), waits=[(evd, ev_xt), (dvd, dvd.n)])
                mm_end(lambda t, c=c, h=h: t.matmul(
                    psD_[:], lms8[:, c * 128:(c + 1) * 128],
                    oneh8f[:, h * 128:(h + 1) * 128], start=False, stop=True,
                    skip_group_check=True))
                E.op("vector", lambda v: v.tensor_add(expd[:], psD_[:], mlow[:]),
                     waits=[(mmd, mmd.n)], inc=(dvd, 1))
                E.op("scalar", lambda s: s.activation(expd[:], expd[:], AF.Exp),
                     waits=[(dvd, dvd.n)], inc=(evd, 1))
                E.op("vector", lambda v: v.tensor_mul(expd[:], expd[:], Gt[:]),
                     waits=[(evd, evd.n)], inc=(dvd, 1))
                gt_dv = dvd.n
                E.wait("tensor", dvd, gt_dv)
                E.op("tensor", lambda t, c=c, h=h: t.matmul(
                    psY_[:, 0:64], expd[:], xst[c][:, h * 64:(h + 1) * 64],
                    start=True, stop=True))
                E.op("tensor", lambda t, c=c, h=h: t.matmul(
                    psC_[:, 0:64], bcg[0:64, 512 + c * 128: 512 + (c + 1) * 128],
                    H_sb[0:64, h * 64:(h + 1) * 64],
                    start=True, stop=True))
                y_mm = mm_end(lambda t: t.nop())
                E.op("scalar", lambda s, c=c, h=h: s.activation(
                    ytok[c][:, h * 64:(h + 1) * 64], psY_[:, 0:64], AF.Copy),
                    waits=[(mmd, y_mm)], inc=(evd, 1))
                E.op("vector", lambda v, c=c, h=h: v.scalar_tensor_tensor(
                    ytok[c][:, h * 64:(h + 1) * 64], psC_[:, 0:64], trc[:, h:h + 1],
                    ytok[c][:, h * 64:(h + 1) * 64], ALU.mult, ALU.add),
                    waits=[(evd, evd.n)], inc=(dvd, 1))
                E.op("vector", lambda v, c=c, h=h: v.scalar_tensor_tensor(
                    ytok[c][:, h * 64:(h + 1) * 64], xst[c][:, h * 64:(h + 1) * 64],
                    dtok[:, h:h + 1], ytok[c][:, h * 64:(h + 1) * 64], ALU.mult, ALU.add),
                    inc=(dvd, 1))
                E.op("vector", lambda v, h=h: v.tensor_scalar_mul(
                    Ihf[:], eyeb64[:], trc[0:64, 16 + h:17 + h]), inc=(dvd, 1))
                E.op("vector", lambda v, h=h: v.tensor_scalar_mul(
                    Zh[:], bt[:], trc[:, 8 + h:9 + h]), inc=(dvd, 1))
                zh_dv = dvd.n
                E.wait("tensor", dvd, zh_dv)
                E.op("tensor", lambda t, h=h, first=first: t.matmul(
                    psH_[0:64, h * 64:(h + 1) * 64],
                    Ihf[:], H_sb[0:64, h * 64:(h + 1) * 64],
                    start=first, stop=False, skip_group_check=True))
                mm_end(lambda t, c=c, h=h: t.matmul(
                    psH_[0:64, h * 64:(h + 1) * 64],
                    Zh[:], xst[c][:, h * 64:(h + 1) * 64],
                    start=False, stop=False, skip_group_check=True))
            E.op("scalar", lambda s: s.activation(H_sb[:], psH_[0:64, :], AF.Copy),
                 waits=[(mmd, mmd.n)], inc=(evd, 1))
            E.wait("tensor", evd, evd.n)
        E.barrier(bar)
        if debug and g == 0:
            for nm, t_ in [("d_ug", ug), ("d_zg", zg), ("d_xsg", xsg), ("d_bcg", bcg),
                           ("d_dt8", dt8), ("d_s8", s8), ("d_trc", trc), ("d_Gt", Gt),
                           ("d_expd", expd), ("d_xst", xst[0]), ("d_yt0", ytok[0]),
                           ("d_H", H_sb)]:
                E.op("sync", lambda s, nm=nm, t_=t_: s.dma_start(out=dbg[nm], in_=t_[:]),
                     inc=(sto, 16))
            E.barrier(bar)
        if stage < 3:
            E.op("sync", lambda s, g=g: s.dma_start(
                out=y_d.rearrange("p (m t) -> p m t", m=8)[:, :, g * 512:(g + 1) * 512],
                in_=yg[:].rearrange("p (m t) -> p m t", m=8)), inc=(sto, 16))
            E.op("sync", lambda s, g=g: s.dma_start(
                out=ysq_d[:, g * 512:(g + 1) * 512], in_=ysqrow[:]), inc=(sto, 16))
            E.barrier(bar)
            continue
        for c in range(NCH):
            for hp in range(4):
                E.op("tensor", lambda t, c=c, hp=hp: t.transpose(
                    psT_[:, 0:128],
                    ytok[c][:, hp * 128:(hp + 1) * 128], eyeb[:]))
                yt_mm = mm_end(lambda t: t.nop())
                E.op("vector", lambda v, c=c, hp=hp: v.tensor_copy(
                    yfm[:, hp * 512 + c * 128: hp * 512 + (c + 1) * 128],
                    psT_[:, 0:128]), waits=[(mmd, yt_mm)], inc=(dvd, 1))
                E.wait("tensor", dvd, dvd.n)
        if stage < 4:
            E.wait("sync", dvd, dvd.n)
            E.op("sync", lambda s, g=g: s.dma_start(
                out=y_d.rearrange("p (m t) -> p m t", m=8)[:, :, g * 512:(g + 1) * 512],
                in_=yg[:].rearrange("p (m t) -> p m t", m=8)), inc=(sto, 16))
            E.op("sync", lambda s, g=g: s.dma_start(
                out=ysq_d[:, g * 512:(g + 1) * 512], in_=ysqrow[:]), inc=(sto, 16))
            E.barrier(bar)
            continue
        for m in range(4):
            E.op("scalar", lambda s, m=m: s.activation(
                sq[:], zg[:, m * 512:(m + 1) * 512], AF.Silu),
                waits=[(dvd, dvd.n), (mmd, mmd.n)], inc=(evd, 1))
            E.op("vector", lambda v, m=m: v.tensor_mul(
                ygat[:, m * 512:(m + 1) * 512], yfm[:, m * 512:(m + 1) * 512], sq[:]),
                waits=[(evd, evd.n)], inc=(dvd, 1))
            E.op("scalar", lambda s, m=m: s.activation(
                sq[:], ygat[:, m * 512:(m + 1) * 512], AF.Square),
                waits=[(dvd, dvd.n)], inc=(evd, 1))
            E.op("tensor", lambda t, m=m: t.matmul(
                psP_[0:1, 0:512], ones_c[:], sq[:], start=(m == 0), stop=(m == 3),
                skip_group_check=True), waits=[(evd, evd.n)],
                **({"inc": (mmd, 1)} if m == 3 else {}))
            E.op("scalar", lambda s, m=m: s.activation(
                ynb[:, m * 512:(m + 1) * 512], ygat[:, m * 512:(m + 1) * 512],
                AF.Copy, scale=nwc[:, m:m + 1]), inc=(evd, 1))
        E.op("vector", lambda v: v.tensor_copy(ysqrow[:], psP_[0:1, 0:512]),
             waits=[(mmd, mmd.n)], inc=(dvd, 1))
        if debug and g == 0:
            E.op("sync", lambda s: s.dma_start(out=dbg["d_yfm"], in_=yfm[:]),
                 waits=[(dvd, dvd.n)], inc=(sto, 16))
            E.op("sync", lambda s: s.dma_start(out=dbg["d_ygat"], in_=ygat[:]),
                 waits=[(dvd, dvd.n)], inc=(sto, 16))
        E.op("sync", lambda s, g=g: s.dma_start(
            out=ysq_d[:, g * 512:(g + 1) * 512], in_=ysqrow[:]),
            waits=[(dvd, dvd.n)], inc=(sto, 16))
        E.wait("tensor", evd, evd.n)
        E.wait("tensor", dvd, dvd.n)
        if stage < 5:
            E.op("sync", lambda s, g=g: s.dma_start(
                out=y_d.rearrange("p (m t) -> p m t", m=8)[:, :, g * 512:(g + 1) * 512],
                in_=yg[:].rearrange("p (m t) -> p m t", m=8)), inc=(sto, 16))
            E.op("sync", lambda s, g=g: s.dma_start(
                out=ysq_d[:, g * 512:(g + 1) * 512], in_=ysqrow[:]),
                waits=[(dvd, dvd.n)], inc=(sto, 16))
            E.barrier(bar)
            continue
        for mp in range(8):
            for k in range(4):
                E.op("tensor", lambda t, mp=mp, k=k: t.matmul(
                    psP_[:, 0:512], wo[:, mp * 512 + k * 128: mp * 512 + (k + 1) * 128],
                    ynb[:, k * 512:(k + 1) * 512], start=(k == 0), stop=(k == 3)),
                    **({"inc": (mmd, 1)} if k == 3 else {}))
            E.op("scalar", lambda s, mp=mp: s.activation(
                yg[:, mp * 512:(mp + 1) * 512], psP_[:, 0:512], AF.Copy),
                waits=[(mmd, mmd.n)], inc=(evd, 1))
            E.wait("tensor", evd, evd.n)
        E.op("sync", lambda s, g=g: s.dma_start(
            out=y_d.rearrange("p (m t) -> p m t", m=8)[:, :, g * 512:(g + 1) * 512],
            in_=yg[:].rearrange("p (m t) -> p m t", m=8)),
            waits=[(evd, evd.n)], inc=(sto, 16))
        E.barrier(bar)

    E.op("sync", lambda s: s.wait_ge(sto.h, sto.n))
    E.replay(block)
    es.close()
    return nc


def stage_L3_inputs(z, b, hg, ngroups=16, x2=None):
    g = lambda k: np.asarray(z[k], np.float32)
    W_in = g("L0_attn_b_W_in")
    WT = np.ascontiguousarray(W_in.T)
    cw = g("L0_attn_b_conv_w")
    cbv = g("L0_attn_b_conv_b")
    zsl = slice(512 * hg, 512 * (hg + 1))
    xsl = slice(2048 + 512 * hg, 2048 + 512 * (hg + 1))
    bcsl = slice(4096, 4224)
    dtsl = slice(4224 + 8 * hg, 4224 + 8 * (hg + 1))
    cw_x = cw[512 * hg: 512 * (hg + 1)]
    cw_bc = cw[2048:2176]
    D = g("L0_attn_b_D")[8 * hg: 8 * (hg + 1)]
    dtok = np.tile(D.reshape(1, 8), (128, 1))
    oh = np.zeros((8, 8, 128), np.float32)
    for e in range(8):
        oh[e, e, :] = 1.0
    WoT = np.ascontiguousarray(g("L0_attn_b_W_out").T)
    return {
        **({"x": fm_layout(np.ascontiguousarray(x2[b].T))} if x2 is not None else {}),
        "wz": col_layout(np.ascontiguousarray(WT[:, zsl])),
        "wx": col_layout(np.ascontiguousarray(WT[:, xsl])),
        "wbc": col_layout(np.ascontiguousarray(
            np.concatenate([WT[:, 4096:4160], np.zeros((1024, 64), np.float32),
                            WT[:, 4160:4224], np.zeros((1024, 64), np.float32)], axis=1))),
        "wdt": col_layout(np.ascontiguousarray(WT[:, dtsl]), tile_m=8),
        "cx": diag_stage(cw_x),
        "cb": diag_stage(np.concatenate(
            [cw_bc[0:64], np.zeros((64, 4), np.float32),
             cw_bc[64:128], np.zeros((64, 4), np.float32)], axis=0)),
        "cxb": vec_cols(cbv[512 * hg: 512 * (hg + 1)]),
        "cbb": vec_cols(np.concatenate(
            [cbv[2048:2112], np.zeros(64, np.float32),
             cbv[2112:2176], np.zeros(64, np.float32)])),
        "dtb": np.ascontiguousarray(g("L0_attn_b_dt_bias")[8 * hg: 8 * (hg + 1)].reshape(8, 1)),
        "alog": np.ascontiguousarray(g("L0_attn_b_A_log")[8 * hg: 8 * (hg + 1)].reshape(8, 1)),
        "dtok": np.ascontiguousarray(dtok),
        "nwc": vec_cols(g("L0_attn_b_norm_w")[512 * hg: 512 * (hg + 1)]),
        "lnw": vec_cols(g("L0_attn_b_ln_w")),
        "lnb": vec_cols(g("L0_attn_b_ln_b")),
        "wo": col_layout(np.ascontiguousarray(WoT[512 * hg: 512 * (hg + 1), :])),
        "mlow": MASK_LOWBIG,
        "eye": EYE128,
        "oneh8": np.ascontiguousarray(oh.transpose(1, 0, 2).reshape(8, 8 * 128)),
        "eyehi": np.ascontiguousarray(
            np.vstack([np.zeros((64, 64), np.float32), np.eye(64, dtype=np.float32)])),
    }


_CACHE = {}


def kernel(**inputs):
    z = inputs
    if "nc1" not in _CACHE:
        _CACHE["nc1"] = build_L1(ngroups=16)
        _CACHE["nc2"] = build_ffn(add_ln_of_base=False)
        _CACHE["nc3"] = build_L3(ngroups=16)
        _CACHE["nc4"] = build_ffn(add_ln_of_base=True, mamba_mode=True)
    wts = stage_ffn_weights(z)

    def fm_slice(a, q):
        return np.ascontiguousarray(
            a.reshape(128, 8, 8192)[:, :, q * 2048:(q + 1) * 2048].reshape(128, 8 * 2048))

    in1 = [stage_L1_inputs(z, c // 4, c % 4, ngroups=16) for c in range(8)]
    r1 = run_bass_kernel_spmd(_CACHE["nc1"], in1, core_ids=list(range(8)))
    aparts = {}
    for c in range(8):
        aparts[(c // 4, c % 4)] = np.asarray(r1.results[c]["ypart"], np.float32)

    x0 = np.asarray(z["L0_attn_a_hidden_states"], np.float32)
    xfm = [fm_layout(np.ascontiguousarray(x0[b].T)) for b in range(2)]
    in2 = []
    for c in range(8):
        b, q = c // 4, c % 4
        m = dict(wts)
        m["xbase"] = fm_slice(xfm[b], q)
        for h in range(4):
            m[f"part{h}"] = fm_slice(aparts[(b, h)], q)
        in2.append(m)
    r2 = run_bass_kernel_spmd(_CACHE["nc2"], in2, core_ids=list(range(8)))
    x2 = np.empty((2, 8192, 1024), np.float32)
    x2fm = {}
    for c in range(8):
        b, q = c // 4, c % 4
        x2fm[(b, q)] = np.asarray(r2.results[c]["xout"], np.float32)
        x2[b, q * 2048:(q + 1) * 2048] = fm_unlayout(x2fm[(b, q)], 1024, 2048).T
    x2fmfull = [np.concatenate(
        [x2fm[(b, q)].reshape(128, 8, 2048) for q in range(4)], axis=2).reshape(128, 8 * 8192)
        for b in range(2)]
    x2fmfull = [np.ascontiguousarray(a) for a in x2fmfull]

    in3 = []
    for c in range(8):
        b, hg = c // 4, c % 4
        m = stage_L3_inputs(z, b, hg, ngroups=16, x2=None)
        m["x"] = x2fmfull[b]
        in3.append(m)
    r3 = run_bass_kernel_spmd(_CACHE["nc3"], in3, core_ids=list(range(8)))
    mparts, ysqs = {}, {}
    for c in range(8):
        mparts[(c // 4, c % 4)] = np.asarray(r3.results[c]["ypart"], np.float32)
        ysqs[(c // 4, c % 4)] = np.asarray(r3.results[c]["ysq"], np.float32)

    in4 = []
    for c in range(8):
        b, q = c // 4, c % 4
        m = dict(wts)
        m["xbase"] = fm_slice(x2fmfull[b], q)
        for hg in range(4):
            m[f"part{hg}"] = fm_slice(mparts[(b, hg)], q)
            m[f"ysq{hg}"] = np.ascontiguousarray(
                ysqs[(b, hg)][:, q * 2048:(q + 1) * 2048])
        in4.append(m)
    r4 = run_bass_kernel_spmd(_CACHE["nc4"], in4, core_ids=list(range(8)))
    out = np.empty((2, 8192, 1024), np.float32)
    for c in range(8):
        b, q = c // 4, c % 4
        out[b, q * 2048:(q + 1) * 2048] = fm_unlayout(
            np.asarray(r4.results[c]["xout"], np.float32), 1024, 2048).T
    return out
